# version 08
# baseline (speedup 1.0000x reference)
import numpy as np
import ml_dtypes
from contextlib import ExitStack
import concourse.bass as bass
import concourse.mybir as mybir
from concourse.bass import IndirectOffsetOnAxis
from concourse.bass_utils import run_bass_kernel_spmd

F32 = mybir.dt.float32
BF16 = mybir.dt.bfloat16
I32 = mybir.dt.int32
AF = mybir.ActivationFunctionType
ALU = mybir.AluOpType
AX = mybir.AxisListType.X

B, S, F, HG, I = 2, 4096, 2048, 512, 5632
TOK = 1024
NF = F // 128
NSC = S // 512
NT = S // 128
GUP = I // 8
DWN = F // 8
NMT = I // 512
NMC = I // 128

LN2C = float(np.log(2.0))
INV_LN2 = float(1.0 / np.log(2.0))
INV_SCALE = float(np.float32(1.0 / np.float64(np.sqrt(32.0))))
EXP_SHIFT = -8.0
RG8 = [[0, 1, 2, 3, 4, 5, 6, 7]]
RG4 = [[0, 1, 2, 3], [4, 5, 6, 7]]

ENGS = ("sync", "tensor", "vector", "scalar", "gpsimd")


class Ctr:
    __slots__ = ("sem", "n", "name")
    def __init__(self, sem, name):
        self.sem = sem
        self.n = 0
        self.name = name


class Prog:
    def __init__(self):
        self.streams = {e: [] for e in ENGS}
        self.meta = {e: [] for e in ENGS}
        self.holder = {}
        self.maxwait = {}

    def em(self, eng, fn, meta=("op",)):
        self.streams[eng].append(fn)
        self.meta[eng].append(meta)

    def wait(self, eng, ctr, val=None):
        v = ctr.n if val is None else val
        if v <= 0:
            return
        key = (eng, ctr.name)
        if v <= self.maxwait.get(key, 0):
            return
        self.maxwait[key] = v
        self.em(eng, lambda e, s=ctr.sem, v=v: e.wait_ge(s, v), ("wait", ctr.name, v))

    def bump(self, eng, ctr, k, fn):
        ctr.n += k
        self.em(eng, lambda e, fn=fn, s=ctr.sem, k=k: fn(e).then_inc(s, k),
                ("bump", ctr.name, k))

    def run(self, eng, fn):
        self.em(eng, fn)

    def vdrain(self):
        self.em("vector", lambda e: e.drain())

    def check(self):
        pos = {e: 0 for e in ENGS}
        cnt = {}
        progressed = True
        while progressed:
            progressed = False
            for e in ENGS:
                while pos[e] < len(self.meta[e]):
                    m = self.meta[e][pos[e]]
                    if m[0] == "wait" and cnt.get(m[1], 0) < m[2]:
                        break
                    if m[0] == "bump":
                        cnt[m[1]] = cnt.get(m[1], 0) + m[2]
                    pos[e] += 1
                    progressed = True
        stuck = {e: (pos[e], len(self.meta[e])) for e in ENGS if pos[e] < len(self.meta[e])}
        if stuck:
            detail = {e: self.meta[e][pos[e]] for e in stuck}
            raise RuntimeError(
                f"static deadlock: stuck={stuck} blockers={detail} counts={cnt}")


def build_program(upto=None):
    nc = bass.Bass(trn_type="TRN2", target_bir_lowering=False, disable_frame_to_traceback=True)

    xb = nc.dram_tensor("xb", [S, F], F32, kind="ExternalInput").ap()
    x_tok = nc.dram_tensor("x_tok", [TOK, F], F32, kind="ExternalInput").ap()
    wq = nc.dram_tensor("wq", [F, HG], F32, kind="ExternalInput").ap()
    wk = nc.dram_tensor("wk", [F, HG], F32, kind="ExternalInput").ap()
    wv = nc.dram_tensor("wv", [F, HG], F32, kind="ExternalInput").ap()
    wo = nc.dram_tensor("wo", [F, F], F32, kind="ExternalInput").ap()
    bqkv = nc.dram_tensor("bqkv", [128, 12], F32, kind="ExternalInput").ap()
    bo = nc.dram_tensor("bo", [1, F], F32, kind="ExternalInput").ap()
    wg = nc.dram_tensor("wg", [I, F], F32, kind="ExternalInput").ap()
    wu = nc.dram_tensor("wu", [I, F], F32, kind="ExternalInput").ap()
    wd_my = nc.dram_tensor("wd_my", [DWN, I], F32, kind="ExternalInput").ap()
    didx_my = nc.dram_tensor("didx_my", [128, 6], I32, kind="ExternalInput").ap()
    y = nc.dram_tensor("y", [TOK, F], F32, kind="ExternalOutput").ap()

    xb16 = nc.dram_tensor("xb16", [S, F], BF16).ap()
    vfm = nc.dram_tensor("vfm", [640, S], BF16).ap()
    gtmp = nc.dram_tensor("gtmp", [GUP, F], BF16).ap()
    utmp = nc.dram_tensor("utmp", [GUP, F], BF16).ap()
    dtmp = nc.dram_tensor("dtmp", [DWN, I], BF16).ap()
    ccg_i = nc.dram_tensor("ccg_i", [F, GUP], BF16).ap()
    ccg_o = nc.dram_tensor("ccg_o", [8 * F, GUP], BF16).ap()
    ccu_i = nc.dram_tensor("ccu_i", [F, GUP], BF16).ap()
    ccu_o = nc.dram_tensor("ccu_o", [8 * F, GUP], BF16).ap()
    ccd_i = nc.dram_tensor("ccd_i", [I, DWN], BF16).ap()
    ccd_o = nc.dram_tensor("ccd_o", [8 * I, DWN], BF16).ap()
    cc2i = nc.dram_tensor("cc2i", [HG, S], BF16).ap()
    cc2o = nc.dram_tensor("cc2o", [F, S], BF16).ap()
    qxd = nc.dram_tensor("qxd", [TOK, F], BF16).ap()
    qtd = nc.dram_tensor("qtd", [TOK, I], BF16).ap()

    p = Prog()
    es = ExitStack()
    sb = lambda *a: es.enter_context(nc.sbuf_tensor(*a))
    psb = lambda *a: es.enter_context(nc.psum_tensor(*a))
    mksem = lambda n: Ctr(es.enter_context(nc.semaphore(n)), n)

    didx_sb = sb("didx_sb", [128, 6], I32)
    bqkv_sb = sb("bqkv_sb", [128, 12], F32)
    bo16 = sb("bo16", [1, F], BF16)
    bo32 = sb("bo32", [1, F], F32)
    c448 = sb("c448", [128, 1], F32)
    bm8 = sb("bm8", [128, 1], F32)
    b9l2 = sb("b9l2", [128, 1], F32)
    bm9l2 = sb("bm9l2", [128, 1], F32)
    ones_col = sb("ones_col", [128, 1], BF16)
    ones_row = sb("ones_row", [1, 128], BF16)
    e2b = sb("e2b", [128, 384], F32)
    qs0 = sb("qs0", [128, 2048], F32)
    qs1 = sb("qs1", [128, 2048], F32)
    qs2 = sb("qs2", [128, 2048], F32)
    qs3 = sb("qs3", [128, 2048], F32)
    qs4 = sb("qs4", [128, 2048], F32)
    qi0 = sb("qi0", [128, 2048], I32)
    rmax = sb("rmax", [128, 1], F32)
    rmax2 = sb("rmax2", [128, 1], F32)
    s448 = sb("s448", [128, 1], F32)
    wsc = sb("wsc", [128, 1], F32)
    lnsc = sb("lnsc", [128, 12], F32)
    sx_all = sb("sx_all", [128, 8], F32)
    st_all = sb("st_all", [128, 8], F32)

    pb = [psb(f"pb{i}", [128, 512], F32) for i in range(8)]

    ld0 = mksem("ld0"); ld1 = mksem("ld1"); ld2 = mksem("ld2"); ld3 = mksem("ld3")
    st0 = mksem("st0"); st1 = mksem("st1")
    vc0 = mksem("vc0"); vc1 = mksem("vc1"); vc2 = mksem("vc2")
    sc0 = mksem("sc0"); sc1 = mksem("sc1")
    mm0 = mksem("mm0"); mm1 = mksem("mm1"); mm2 = mksem("mm2")
    cc0 = mksem("cc0"); gg0 = mksem("gg0"); out0 = mksem("out0"); fq0 = mksem("fq0")

    def _finish():
        p.check()
        with nc.Block() as block:
            @block.sync
            def _(sync):
                for fn in p.streams["sync"]:
                    fn(sync)

            @block.tensor
            def _(tensor):
                for fn in p.streams["tensor"]:
                    fn(tensor)

            @block.vector
            def _(vector):
                for fn in p.streams["vector"]:
                    fn(vector)

            @block.scalar
            def _(scalar):
                for fn in p.streams["scalar"]:
                    fn(scalar)

            @block.gpsimd
            def _(gpsimd):
                for fn in p.streams["gpsimd"]:
                    fn(gpsimd)

        if not p.holder.get("_skip_es_close"):
            es.close()
        return nc

    def _trunc(ctr, val):
        p.wait("gpsimd", ctr, val)
        p.bump("gpsimd", out0, 16, lambda e: e.dma_start(out=y[0:128, :], in_=qs0[:]))
        p.wait("gpsimd", out0, out0.n)
        p.holder["_skip_es_close"] = True
        return _finish()

    p.bump("sync", ld0, 16, lambda e: e.dma_start(out=didx_sb[:], in_=didx_my))
    p.bump("sync", ld0, 16, lambda e: e.dma_start(out=bqkv_sb[:], in_=bqkv))
    p.bump("sync", ld0, 16, lambda e: e.dma_start(out=bo32[:], in_=bo))
    for fn in (
        lambda e: e.memset(c448[:], 448.0),
        lambda e: e.memset(bm8[:], EXP_SHIFT),
        lambda e: e.memset(b9l2[:], 9.0 * LN2C),
        lambda e: e.memset(bm9l2[:], -9.0 * LN2C),
        lambda e: e.memset(ones_col[:], 1.0),
        lambda e: e.memset(ones_row[:], 1.0),
        lambda e: e.memset(e2b[0:1, 0:64], 1.0),
        lambda e: e.memset(e2b[0:1, 64:128], 0.0),
        lambda e: e.memset(e2b[0:1, 128:192], 0.0),
        lambda e: e.memset(e2b[0:1, 192:256], 1.0),
        lambda e: e.memset(e2b[0:1, 256:288], 0.0),
        lambda e: e.memset(e2b[0:1, 288:352], 1.0),
        lambda e: e.memset(e2b[0:1, 352:384], 0.0),
    ):
        p.run("vector", fn)
    p.wait("vector", ld0)
    p.bump("vector", vc0, 1, lambda e: e.tensor_copy(bo16[:], bo32[:]))
    p.vdrain()

    def emit_rowscale(rows, src_ap, save_scale=None):
        p.wait("vector", sc0, sc0.n)
        p.run("vector", lambda e, s=src_ap: e.tensor_reduce(
            rmax[0:rows, :], s, AX, ALU.max, apply_absolute_value=True))
        p.vdrain()
        p.run("vector", lambda e: e.reciprocal(s448[0:rows, :], rmax[0:rows, :]))
        p.vdrain()
        p.run("vector", lambda e: e.tensor_scalar_mul(s448[0:rows, :], s448[0:rows, :], 448.0))
        p.vdrain()
        if save_scale is not None:
            p.run("vector", lambda e, d=save_scale: e.tensor_scalar_mul(
                d, rmax[0:rows, :], 1.0 / 448.0))
            p.vdrain()
        p.bump("vector", vc0, 1, lambda e: e.engine_nop())

    def emit_quant(rows, width, src_ap, dst_ap):
        a0 = qs0[0:rows, 0:width]
        a1 = qs1[0:rows, 0:width]
        a2 = qs2[0:rows, 0:width]
        a4 = qs4[0:rows, 0:width]
        ii = qi0[0:rows, 0:width]
        sdr = lambda: p.em("scalar", lambda e: e.drain())
        p.wait("scalar", vc0)
        p.bump("scalar", sc0, 1, lambda e, s=src_ap: e.activation(
            a0, s, AF.Identity, scale=s448[0:rows, :]))
        sdr()
        p.bump("scalar", sc0, 1, lambda e: e.activation(a1, a0, AF.Abs))
        p.wait("vector", sc0)
        p.bump("vector", vc0, 1, lambda e: e.tensor_scalar_max(a2, a1, 2.0 ** -6))
        p.vdrain()
        p.wait("scalar", vc0)
        p.bump("scalar", sc0, 1, lambda e: e.activation(a1, a2, AF.Ln))
        p.wait("vector", sc0)
        p.run("vector", lambda e: e.tensor_scalar(a2, a1, INV_LN2, 5.5, ALU.mult, ALU.add))
        p.vdrain()
        p.run("vector", lambda e: e.tensor_copy(ii, a2))
        p.vdrain()
        p.bump("vector", vc0, 1, lambda e: e.tensor_copy(a2, ii))
        p.vdrain()
        p.wait("scalar", vc0)
        p.bump("scalar", sc0, 1, lambda e: e.activation(
            a1, a2, AF.Exp, bias=b9l2[0:rows, :], scale=-LN2C))
        sdr()
        p.bump("scalar", sc0, 1, lambda e: e.activation(
            a4, a2, AF.Exp, bias=bm9l2[0:rows, :], scale=LN2C))
        p.wait("vector", sc0)
        p.run("vector", lambda e: e.tensor_mul(a2, a0, a1))
        p.vdrain()
        p.run("vector", lambda e: e.tensor_copy(ii, a2))
        p.vdrain()
        p.run("vector", lambda e: e.tensor_copy(a2, ii))
        p.vdrain()
        p.bump("vector", vc0, 1, lambda e, d=dst_ap: e.tensor_mul(d, a2, a4))
        p.vdrain()

    def emit_ln(src_ap, dst_ap):
        p.run("vector", lambda e, s=src_ap: e.scalar_tensor_tensor(
            qs1[:, 0:F], s, 1.0, s, ALU.mult, ALU.mult, accum_out=lnsc[:, 4:5]))
        p.vdrain()
        p.run("vector", lambda e: e.tensor_add(lnsc[:, 5:6], lnsc[:, 0:1], lnsc[:, 1:2]))
        p.vdrain()
        p.run("vector", lambda e: e.tensor_add(lnsc[:, 6:7], lnsc[:, 2:3], lnsc[:, 3:4]))
        p.vdrain()
        p.run("vector", lambda e: e.tensor_add(lnsc[:, 7:8], lnsc[:, 5:6], lnsc[:, 6:7]))
        p.vdrain()
        p.run("vector", lambda e: e.tensor_scalar_mul(lnsc[:, 8:9], lnsc[:, 7:8], 1.0 / F))
        p.vdrain()
        p.run("vector", lambda e: e.tensor_scalar_mul(lnsc[:, 9:10], lnsc[:, 4:5], 1.0 / F))
        p.vdrain()
        p.run("vector", lambda e: e.tensor_tensor(lnsc[:, 10:11], lnsc[:, 8:9], lnsc[:, 8:9], ALU.mult))
        p.vdrain()
        p.run("vector", lambda e: e.tensor_sub(lnsc[:, 9:10], lnsc[:, 9:10], lnsc[:, 10:11]))
        p.vdrain()
        p.bump("vector", vc0, 1, lambda e: e.tensor_scalar_add(lnsc[:, 9:10], lnsc[:, 9:10], 1e-5))
        p.vdrain()
        p.wait("scalar", vc0)
        p.bump("scalar", sc0, 1, lambda e: e.activation(lnsc[:, 10:11], lnsc[:, 9:10], AF.Ln))
        p.wait("vector", sc0)
        p.bump("vector", vc0, 1, lambda e: e.engine_nop())
        p.wait("scalar", vc0)
        p.bump("scalar", sc0, 1, lambda e: e.activation(
            lnsc[:, 11:12], lnsc[:, 10:11], AF.Exp, scale=-0.5))
        p.wait("vector", sc0)
        p.bump("vector", vc0, 1, lambda e: e.scalar_tensor_tensor(
            lnsc[:, 10:11], lnsc[:, 8:9], -1.0, lnsc[:, 11:12], ALU.mult, ALU.mult))
        p.vdrain()
        p.wait("scalar", vc0)
        p.bump("scalar", sc0, 1, lambda e, s=src_ap, d=dst_ap: e.activation(
            d, s, AF.Identity, bias=lnsc[:, 10:11], scale=lnsc[:, 11:12]))

    def emit_p1_chunk(src_dram, tmp_dram, j, wrX, qrX):
        rows = 128 if j < 5 else 64
        r0 = j * 128
        p.wait("gpsimd", ld0)
        p.wait("gpsimd", vc0, vc0.n)
        p.wait("gpsimd", st1, st1.n)
        p.bump("gpsimd", gg0, 16, lambda e, rows=rows, j=j, src=src_dram, wrX=wrX: e.indirect_dma_start(
            wrX[0:rows, 0:F], None, src,
            IndirectOffsetOnAxis(ap=didx_sb[0:rows, j:j + 1], axis=0)))
        p.wait("vector", gg0)
        emit_rowscale(rows, wrX[0:rows, 0:F], save_scale=wsc[0:rows, :])
        emit_quant(rows, F, wrX[0:rows, 0:F], qs3[0:rows, 0:F])
        p.wait("vector", st1, st1.n)
        p.bump("vector", fq0, 1, lambda e, rows=rows, qrX=qrX: e.tensor_scalar(
            qrX[0:rows, 0:F], qs3[0:rows, 0:F], wsc[0:rows, 0:1], None, ALU.mult))
        p.vdrain()
        p.wait("gpsimd", fq0)
        p.bump("gpsimd", st1, 16, lambda e, rows=rows, r0=r0, t=tmp_dram, qrX=qrX: e.dma_start(
            out=t[r0:r0 + rows, :], in_=qrX[0:rows, 0:F]))

    WDC = [(0, 2048), (2048, 4096), (4096, I)]

    def emit_wd(j, wrX, qrX):
        p.wait("sync", ld1, ld1.n)
        for c, (w0, w1) in enumerate(WDC):
            p.wait("sync", vc0, vc0.n)
            p.bump("sync", ld1, 16, lambda e, j=j, w0=w0, w1=w1, wrX=wrX: e.dma_start(
                out=wrX[:, 0:w1 - w0], in_=wd_my[j * 128:(j + 1) * 128, w0:w1]))
            p.wait("vector", ld1)
            if c == 0:
                p.wait("vector", sc0, sc0.n)
                p.bump("vector", vc0, 1, lambda e, w0=w0, w1=w1, wrX=wrX: e.tensor_reduce(
                    rmax[:, :], wrX[:, 0:w1 - w0], AX, ALU.max, apply_absolute_value=True))
            else:
                p.run("vector", lambda e, w0=w0, w1=w1, wrX=wrX: e.tensor_reduce(
                    rmax2[:, :], wrX[:, 0:w1 - w0], AX, ALU.max, apply_absolute_value=True))
                p.vdrain()
                p.bump("vector", vc0, 1, lambda e: e.tensor_tensor(
                    rmax[:, :], rmax[:, :], rmax2[:, :], ALU.max))
            p.vdrain()
        p.run("vector", lambda e: e.reciprocal(s448[:, :], rmax[:, :]))
        p.vdrain()
        p.run("vector", lambda e: e.tensor_scalar_mul(s448[:, :], s448[:, :], 448.0))
        p.vdrain()
        p.run("vector", lambda e: e.tensor_scalar_mul(wsc[:, :], rmax[:, :], 1.0 / 448.0))
        p.vdrain()
        p.bump("vector", vc0, 1, lambda e: e.engine_nop())
        for c, (w0, w1) in enumerate(WDC):
            p.wait("sync", vc0, vc0.n)
            p.bump("sync", ld1, 16, lambda e, j=j, w0=w0, w1=w1, wrX=wrX: e.dma_start(
                out=wrX[:, 0:w1 - w0], in_=wd_my[j * 128:(j + 1) * 128, w0:w1]))
            p.wait("vector", ld1)
            p.bump("vector", vc0, 1, lambda e: e.engine_nop())
            emit_quant(128, w1 - w0, wrX[:, 0:w1 - w0], qs3[:, 0:w1 - w0])
            p.wait("vector", st1, st1.n)
            p.bump("vector", fq0, 1, lambda e, w0=w0, w1=w1, qrX=qrX: e.tensor_scalar(
                qrX[:, 0:w1 - w0], qs3[:, 0:w1 - w0], wsc[:, 0:1], None, ALU.mult))
            p.vdrain()
            p.wait("gpsimd", fq0)
            p.bump("gpsimd", st1, 16, lambda e, j=j, w0=w0, w1=w1, qrX=qrX: e.dma_start(
                out=dtmp[j * 128:(j + 1) * 128, w0:w1], in_=qrX[:, 0:w1 - w0]))

    def emit_fc_part(tmp_dram, cci, fcs, ttX):
        for fc in fcs:
            p.wait("sync", st1)
            p.wait("sync", st0, st0.n)
            p.bump("sync", ld1, 16, lambda e, fc=fc, t=tmp_dram, ttX=ttX: e.dma_start(
                out=ttX[:, 0:GUP], in_=t[:, fc * 128:(fc + 1) * 128], transpose=True))
            p.wait("gpsimd", ld1)
            p.bump("gpsimd", st0, 16, lambda e, fc=fc, cci=cci, ttX=ttX: e.dma_start(
                out=cci[fc * 128:(fc + 1) * 128, :], in_=ttX[:, 0:GUP]))

    with nc.sbuf_tensor("xc32", [128, F], F32) as xc32, \
         nc.sbuf_tensor("xc16", [128, F], BF16) as xc16:
        p.bump("vector", vc1, 1, lambda e: e.memset(xc16[:], 0.0))
        p.vdrain()
        p.wait("gpsimd", vc1)
        for cw in range(2):
            p.bump("gpsimd", st1, 16, lambda e, cw=cw: e.dma_start(
                out=vfm[520:640, cw * F:(cw + 1) * F], in_=xc16[0:120, :]))
        p.wait("vector", st1, st1.n)
        p.bump("vector", vc1, 1, lambda e: e.memset(xc16[0:1, :], 1.0))
        p.vdrain()
        p.wait("gpsimd", vc1)
        for cp4 in range(4):
            for rr in (130 * cp4 + 64, 130 * cp4 + 129):
                for cw in range(2):
                    p.bump("gpsimd", st1, 16, lambda e, rr=rr, cw=cw: e.dma_start(
                        out=vfm[rr:rr + 1, cw * F:(cw + 1) * F], in_=xc16[0:1, :]))
        p.wait("sync", ld1)
        p.wait("sync", st0, st0.n)
        p.wait("sync", st1)
        for ch in range(S // 128):
            p.wait("sync", vc1, vc1.n)
            p.bump("sync", ld1, 16, lambda e, ch=ch: e.dma_start(
                out=xc32[:], in_=xb[ch * 128:(ch + 1) * 128, :]))
            p.wait("vector", ld1)
            p.wait("vector", st1, st1.n)
            p.bump("vector", vc1, 1, lambda e: e.tensor_copy(xc16[:], xc32[:]))
            p.vdrain()
            p.wait("gpsimd", vc1)
            p.bump("gpsimd", st1, 16, lambda e, ch=ch: e.dma_start(
                out=xb16[ch * 128:(ch + 1) * 128, :], in_=xc16[:]))
    if upto == 2:
        return _trunc(st1, st1.n)

    with nc.sbuf_tensor("qfm", [128, 4 * S], BF16) as qfm, \
         nc.sbuf_tensor("kfm", [128, 4 * S], BF16) as kfm:

        with nc.sbuf_tensor("w3f", [128, HG], F32) as w3f, \
             nc.sbuf_tensor("wq16", [128, NF * 512], BF16) as wq16, \
             nc.sbuf_tensor("wk16", [128, NF * 512], BF16) as wk16, \
             nc.sbuf_tensor("wv16", [128, NF * 512], BF16) as wv16, \
             nc.sbuf_tensor("xfm", [128, NF * 512], BF16) as xfm, \
             nc.sbuf_tensor("vtile", [128, 512], BF16) as vtile:

            p.wait("sync", st1)
            for wsrc, wdst in ((wq, wq16), (wk, wk16), (wv, wv16)):
                for ci in range(NF):
                    p.wait("sync", vc1, vc1.n)
                    p.bump("sync", ld1, 16, lambda e, ci=ci, wsrc=wsrc: e.dma_start(
                        out=w3f[:], in_=wsrc[ci * 128:(ci + 1) * 128, :]))
                    p.wait("vector", ld1)
                    p.bump("vector", vc1, 1, lambda e, ci=ci, wdst=wdst: e.tensor_copy(
                        wdst[:, ci * 512:(ci + 1) * 512], w3f[:]))
                    p.vdrain()

            for sc in range(NSC):
                p.wait("sync", mm0, mm0.n)
                p.wait("sync", vc1, vc1.n)
                for fc in range(NF):
                    p.bump("sync", ld2, 16, lambda e, fc=fc, sc=sc: e.dma_start(
                        out=xfm[:, fc * 512:(fc + 1) * 512],
                        in_=xb16[sc * 512:(sc + 1) * 512, fc * 128:(fc + 1) * 128],
                        transpose=True))
                p.wait("tensor", ld2)
                p.wait("tensor", vc1, vc1.n)
                for pi, wtile in enumerate((wq16, wk16, wv16)):
                    for j in range(4):
                        psx = pb[(pi * 4 + j) % 3]
                        p.wait("tensor", sc1, sc1.n)
                        for ci in range(NF):
                            p.bump("tensor", mm0, 1, lambda e, psx=psx, ci=ci, j=j, wtile=wtile: e.matmul(
                                psx[:], wtile[:, ci * 512 + j * 128: ci * 512 + j * 128 + 128],
                                xfm[:, ci * 512:(ci + 1) * 512],
                                start=(ci == 0), stop=(ci == NF - 1)))
                        p.wait("scalar", mm0)
                        bia = bqkv_sb[:, pi * 4 + j: pi * 4 + j + 1]
                        if pi == 0:
                            p.bump("scalar", sc1, 1, lambda e, psx=psx, j=j, sc=sc, bia=bia: e.activation(
                                qfm[:, j * S + sc * 512: j * S + sc * 512 + 512], psx[:], AF.Identity, bias=bia))
                        elif pi == 1:
                            p.bump("scalar", sc1, 1, lambda e, psx=psx, j=j, sc=sc, bia=bia: e.activation(
                                kfm[:, j * S + sc * 512: j * S + sc * 512 + 512], psx[:], AF.Identity, bias=bia))
                        else:
                            p.wait("scalar", st1, st1.n)
                            p.bump("scalar", sc1, 1, lambda e, psx=psx, bia=bia: e.activation(
                                vtile[:], psx[:], AF.Identity, bias=bia))
                            p.wait("gpsimd", sc1)
                            p.bump("gpsimd", st1, 16, lambda e, j=j, sc=sc: e.dma_start(
                                out=vfm[j * 130:j * 130 + 64, sc * 512:(sc + 1) * 512],
                                in_=vtile[0:64, :]))
                            p.bump("gpsimd", st1, 16, lambda e, j=j, sc=sc: e.dma_start(
                                out=vfm[j * 130 + 65:j * 130 + 129, sc * 512:(sc + 1) * 512],
                                in_=vtile[64:128, :]))
        if upto == 3:
            return _trunc(st1, st1.n)

        with nc.sbuf_tensor("vtm", [128, NT * 640], BF16) as vtm, \
             nc.sbuf_tensor("pt", [128, 4 * 512], BF16) as pt, \
             nc.sbuf_tensor("rd4", [128, 2048], F32) as rd4, \
             nc.sbuf_tensor("rdz", [1, 4 * 512], F32) as rdz, \
             nc.sbuf_tensor("bc2", [128, 1024], F32) as bc2, \
             nc.sbuf_tensor("fn0", [128, 1024], BF16) as fn0, \
             nc.sbuf_tensor("wrows2", [128, F], F32) as wrows2, \
             nc.sbuf_tensor("qrows2", [128, F], BF16) as qrows2, \
             nc.sbuf_tensor("ttile2", [128, GUP], BF16) as ttile2:

            p.wait("sync", st1)
            for tt in range(NT):
                p.bump("sync", ld3, 16, lambda e, tt=tt: e.dma_start(
                    out=vtm[:, tt * 640:(tt + 1) * 640],
                    in_=vfm[:, tt * 128:(tt + 1) * 128],
                    transpose=True))
            p.wait("tensor", ld3)
            p.wait("tensor", sc1, sc1.n)
            p.wait("tensor", vc0, vc0.n)

            wg_fc_parts = [[0, 1, 2], [3, 4, 5], [6, 7, 8], [9, 10, 11], [12, 13], [14, 15]]
            exp_after = {}
            bc_free = 0
            for sc in range(NSC):
                for qd in range(2):
                    k = sc * 2 + qd
                    if k < 6:
                        emit_p1_chunk(wg, gtmp, k, wrows2, qrows2)
                    elif k < 12:
                        emit_p1_chunk(wu, utmp, k - 6, wrows2, qrows2)
                        emit_fc_part(gtmp, ccg_i, wg_fc_parts[k - 6], ttile2)
                    elif k == 12:
                        emit_wd(0, wrows2, qrows2)
                    elif k == 13:
                        emit_wd(1, wrows2, qrows2)
                    elif k == 14:
                        emit_fc_part(utmp, ccu_i, list(range(NF)), ttile2)
                    av_free = vc2.n
                    for tt in range(NT):
                        for pr in range(2):
                            cp = 2 * qd + pr
                            sA = pb[2 * pr]
                            sB = pb[2 * pr + 1]
                            if pr in exp_after:
                                p.wait("tensor", sc1, exp_after[pr])
                            if tt == 0 and pr == 0:
                                p.wait("tensor", vc1, bc_free)
                            kbase = cp * S + tt * 128
                            qbase = cp * S + sc * 512
                            p.bump("tensor", mm0, 1, lambda e, sA=sA, kbase=kbase, qbase=qbase: e.matmul(
                                sA[:], kfm[0:64, kbase:kbase + 128], qfm[0:64, qbase:qbase + 512],
                                start=True, stop=True, tile_position=(0, 0)))
                            p.bump("tensor", mm0, 1, lambda e, sB=sB, kbase=kbase, qbase=qbase: e.matmul(
                                sB[:], kfm[64:128, kbase:kbase + 128], qfm[64:128, qbase:qbase + 512],
                                start=True, stop=True, tile_position=(64, 0)))
                            p.wait("scalar", mm0)
                            p.wait("scalar", mm1, mm1.n)
                            p.wait("scalar", mm2, mm2.n)
                            p.bump("scalar", sc1, 1, lambda e, sA=sA, pr=pr: e.activation(
                                pt[:, (2 * pr) * 512:(2 * pr) * 512 + 512], sA[:], AF.Exp,
                                bias=bm8[:], scale=INV_SCALE))
                            p.bump("scalar", sc1, 1, lambda e, sB=sB, pr=pr: e.activation(
                                pt[:, (2 * pr + 1) * 512:(2 * pr + 1) * 512 + 512], sB[:], AF.Exp,
                                bias=bm8[:], scale=INV_SCALE))
                            exp_after[pr] = sc1.n
                            if upto == 41 and sc == 0 and qd == 0 and tt == 0 and pr == 0:
                                return _trunc(sc1, sc1.n)
                            p.wait("tensor", sc1, exp_after[pr])
                            if tt == 0:
                                p.wait("tensor", vc2, av_free)
                            avA = pb[4 + 2 * pr]
                            avB = pb[5 + 2 * pr]
                            vbase = tt * 640 + cp * 130
                            p.bump("tensor", mm1, 1, lambda e, avA=avA, vbase=vbase, pr=pr, tt=tt: e.matmul(
                                avA[0:65, :], vtm[:, vbase:vbase + 65],
                                pt[:, (2 * pr) * 512:(2 * pr) * 512 + 512],
                                start=(tt == 0), stop=(tt == NT - 1), tile_position=(0, 0)))
                            p.bump("tensor", mm1, 1, lambda e, avB=avB, vbase=vbase, pr=pr, tt=tt: e.matmul(
                                avB[0:65, :], vtm[:, vbase + 65:vbase + 130],
                                pt[:, (2 * pr + 1) * 512:(2 * pr + 1) * 512 + 512],
                                start=(tt == 0), stop=(tt == NT - 1), tile_position=(0, 0)))
                            if upto == 42 and sc == 0 and qd == 0 and tt == 0 and pr == 1:
                                return _trunc(mm1, mm1.n)
                    if upto == 440 and sc == 0 and qd == 0:
                        return _trunc(mm1, mm1.n)
                    p.wait("vector", mm1, mm1.n)
                    p.wait("vector", ld3, ld3.n)
                    for hh in range(4):
                        bank = pb[4 + hh]
                        p.run("vector", lambda e, bank=bank, hh=hh: e.reciprocal(
                            rd4[64:65, hh * 512:(hh + 1) * 512], bank[64:65, :]))
                        p.vdrain()
                    p.bump("vector", vc2, 1, lambda e: e.engine_nop())
                    if upto == 441 and sc == 0 and qd == 0:
                        return _trunc(vc2, vc2.n)
                    p.wait("sync", vc2, vc2.n)
                    p.wait("sync", mm2, mm2.n)
                    for hh in range(4):
                        p.bump("sync", ld3, 16, lambda e, hh=hh: e.dma_start(
                            out=rdz[0:1, hh * 512:(hh + 1) * 512],
                            in_=rd4[64:65, hh * 512:(hh + 1) * 512]))
                    for pr in range(2):
                        p.wait("tensor", ld3)
                        p.wait("tensor", vc1, vc1.n)
                        p.wait("tensor", sc1, sc1.n)
                        p.bump("tensor", mm2, 1, lambda e, pr=pr: e.matmul(
                            pb[0][:], e2b[0:1, 0:128], rdz[0:1, (2 * pr) * 512:(2 * pr) * 512 + 512],
                            start=True, stop=True, tile_position=(0, 0)))
                        p.bump("tensor", mm2, 1, lambda e, pr=pr: e.matmul(
                            pb[1][:], e2b[0:1, 0:128], rdz[0:1, (2 * pr + 1) * 512:(2 * pr + 1) * 512 + 512],
                            start=True, stop=True, tile_position=(0, 0)))
                        p.wait("vector", mm2)
                        p.wait("vector", st1, st1.n)
                        ca = 0
                        cb = 512
                        p.bump("vector", vc1, 1, lambda e, ca=ca: e.tensor_copy(
                            bc2[0:64, ca:ca + 512], pb[0][0:64, :]))
                        p.vdrain()
                        p.bump("vector", vc1, 1, lambda e, cb=cb: e.tensor_copy(
                            bc2[0:64, cb:cb + 512], pb[1][0:64, :]))
                        p.vdrain()
                        p.bump("vector", vc2, 1, lambda e, pr=pr, ca=ca: e.tensor_tensor(
                            fn0[0:64, ca:ca + 512], pb[4 + 2 * pr][0:64, :],
                            bc2[0:64, ca:ca + 512], ALU.mult))
                        p.vdrain()
                        p.bump("vector", vc2, 1, lambda e, pr=pr, cb=cb: e.tensor_tensor(
                            fn0[0:64, cb:cb + 512], pb[5 + 2 * pr][0:64, :],
                            bc2[0:64, cb:cb + 512], ALU.mult))
                        p.vdrain()
                        if upto == 442 and sc == 0 and qd == 0 and pr == 0:
                            return _trunc(vc1, vc1.n)
                        p.wait("gpsimd", vc2)
                        strip = (2 * qd + pr) * 128
                        p.bump("gpsimd", st1, 16, lambda e, strip=strip, sc=sc, ca=ca: e.dma_start(
                            out=cc2i[strip:strip + 64, sc * 512:(sc + 1) * 512],
                            in_=fn0[0:64, ca:ca + 512]))
                        p.bump("gpsimd", st1, 16, lambda e, strip=strip, sc=sc, cb=cb: e.dma_start(
                            out=cc2i[strip + 64:strip + 128, sc * 512:(sc + 1) * 512],
                            in_=fn0[0:64, cb:cb + 512]))
                    bc_free = vc1.n
                    if upto == 44 and sc == 0 and qd == 0:
                        return _trunc(st1, st1.n)

            p.wait("gpsimd", st1)
            p.bump("gpsimd", cc0, 1, lambda e: e.collective_compute(
                "AllGather", ALU.bypass, replica_groups=RG4, ins=[cc2i], outs=[cc2o]))
            if upto == 4:
                return _trunc(cc0, 1)

            p.wait("gpsimd", st0, st0.n)
            p.bump("gpsimd", cc0, 1, lambda e: e.collective_compute(
                "AllGather", ALU.bypass, replica_groups=RG8, ins=[ccg_i], outs=[ccg_o]))
            p.bump("gpsimd", cc0, 1, lambda e: e.collective_compute(
                "AllGather", ALU.bypass, replica_groups=RG8, ins=[ccu_i], outs=[ccu_o]))

            for mc in range(NMC):
                p.wait("sync", st1)
                p.wait("sync", st0, st0.n)
                p.bump("sync", ld1, 16, lambda e, mc=mc: e.dma_start(
                    out=ttile2[:, 0:DWN], in_=dtmp[:, mc * 128:(mc + 1) * 128], transpose=True))
                p.wait("gpsimd", ld1)
                p.bump("gpsimd", st0, 16, lambda e, mc=mc: e.dma_start(
                    out=ccd_i[mc * 128:(mc + 1) * 128, :], in_=ttile2[:, 0:DWN]))
            p.wait("gpsimd", st0)
            p.bump("gpsimd", cc0, 1, lambda e: e.collective_compute(
                "AllGather", ALU.bypass, replica_groups=RG8, ins=[ccd_i], outs=[ccd_o]))

    x1s = sb("x1s", [128, 8 * F], BF16)
    with nc.sbuf_tensor("ffm", [128, NF * 512], BF16) as ffm, \
         nc.sbuf_tensor("wof", [128, 512], F32) as wof, \
         nc.sbuf_tensor("wo16", [128, NF * F], BF16) as wo16, \
         nc.sbuf_tensor("xtokt", [128, F], F32) as xtokt, \
         nc.sbuf_tensor("x1t", [128, F], F32) as x1t, \
         nc.sbuf_tensor("x1f", [128, F], F32) as x1f, \
         nc.sbuf_tensor("qx16", [128, F], BF16) as qx16:

        p.wait("sync", st1)
        p.wait("sync", st0, st0.n)
        for ci in range(NF):
            for half in range(4):
                p.wait("sync", vc1, vc1.n)
                p.bump("sync", ld1, 16, lambda e, ci=ci, half=half: e.dma_start(
                    out=wof[:], in_=wo[ci * 128:(ci + 1) * 128, half * 512:(half + 1) * 512]))
                p.wait("vector", ld1)
                p.bump("vector", vc1, 1, lambda e, ci=ci, half=half: e.tensor_copy(
                    wo16[:, ci * F + half * 512: ci * F + (half + 1) * 512], wof[:]))
                p.vdrain()

        def _mk_rank(e):
            p.holder["rank"] = e.cc_rank(RG4)
        p.em("sync", _mk_rank)
        p.wait("sync", cc0, 1)
        cc2o_r3 = cc2o.rearrange("(a p) s -> p a s", p=128)
        for h5 in range(2):
            p.wait("sync", mm0, mm0.n)
            p.bump("sync", ld2, 16, lambda e, h5=h5: e.dma_start(
                out=ffm.rearrange("p (a c) -> p a c", a=NF),
                in_=cc2o_r3[:, :, bass.ts(p.holder["rank"] * 2 + h5, 512)]))
            for ss in range(4):
                srow = h5 * 512 + ss * 128
                sidx = h5 * 4 + ss
                p.wait("sync", vc1, vc1.n)
                p.bump("sync", ld1, 16, lambda e, srow=srow: e.dma_start(
                    out=xtokt[:], in_=x_tok[srow:srow + 128, :]))
                p.wait("tensor", ld2)
                for j in range(4):
                    psx = pb[j % 2]
                    p.wait("tensor", vc1, vc1.n)
                    for ci in range(NF):
                        p.bump("tensor", mm0, 1, lambda e, psx=psx, ci=ci, j=j, ss=ss: e.matmul(
                            psx[:], ffm[:, ci * 512 + ss * 128: ci * 512 + ss * 128 + 128],
                            wo16[:, ci * F + j * 512: ci * F + (j + 1) * 512],
                            start=(ci == 0), stop=False))
                    p.bump("tensor", mm0, 1, lambda e, psx=psx, j=j: e.matmul(
                        psx[:], ones_row[:], bo16[0:1, j * 512:(j + 1) * 512],
                        start=False, stop=True))
                    p.wait("vector", mm0)
                    p.wait("vector", ld1)
                    p.wait("vector", sc0, sc0.n)
                    p.bump("vector", vc1, 1, lambda e, psx=psx, j=j: e.scalar_tensor_tensor(
                        x1t[:, j * 512:(j + 1) * 512], psx[:], 1.0, xtokt[:, j * 512:(j + 1) * 512],
                        ALU.mult, ALU.add, accum_out=lnsc[:, j:j + 1]))
                    p.vdrain()
                emit_ln(x1t[:], x1f[:])
                p.wait("vector", sc0)
                p.bump("vector", vc1, 1, lambda e, sidx=sidx: e.tensor_copy(
                    x1s[:, sidx * F:(sidx + 1) * F], x1f[:]))
                p.vdrain()
                emit_rowscale(128, x1f[:], save_scale=sx_all[:, sidx:sidx + 1])
                p.wait("vector", st1, st1.n)
                emit_quant(128, F, x1f[:], qx16[:])
                p.wait("gpsimd", vc0)
                p.bump("gpsimd", st1, 16, lambda e, srow=srow: e.dma_start(
                    out=qxd[srow:srow + 128, :], in_=qx16[:]))
    if upto == 5:
        return _trunc(st1, st1.n)

    def seg_iter(m0, m1):
        out = []
        while m0 < m1:
            s_ = m0 // GUP
            e_ = min(m1, (s_ + 1) * GUP)
            out.append((s_, m0 - s_ * GUP, e_ - m0))
            m0 = e_
        return out

    for h6 in range(2):
        with nc.sbuf_tensor(f"qxf{h6}", [128, NF * 512], BF16) as qxf, \
             nc.sbuf_tensor(f"grhs{h6}", [128, NF * 512], BF16) as grhs, \
             nc.sbuf_tensor(f"urhs{h6}", [128, NF * 512], BF16) as urhs, \
             nc.sbuf_tensor(f"gts{h6}", [128, 512], BF16) as gts, \
             nc.sbuf_tensor(f"uts{h6}", [128, 512], BF16) as uts, \
             nc.sbuf_tensor(f"tsb{h6}", [128, 4 * I], BF16) as tsb, \
             nc.sbuf_tensor(f"qt16_{h6}", [128, I], BF16) as qt16:

            if h6 == 0:
                p.wait("tensor", cc0, 3)
            p.wait("sync", cc0, 3)
            p.wait("sync", st1)
            p.wait("sync", vc1, vc1.n)
            p.wait("sync", mm2, mm2.n)
            p.wait("sync", mm1, mm1.n)
            for fc in range(NF):
                p.bump("sync", ld2, 16, lambda e, fc=fc, h6=h6, qxf=qxf: e.dma_start(
                    out=qxf[:, fc * 512:(fc + 1) * 512],
                    in_=qxd[h6 * 512:(h6 + 1) * 512, fc * 128:(fc + 1) * 128],
                    transpose=True))
            for mt in range(NMT):
                p.wait("sync", mm1, mm1.n)
                segs = seg_iter(mt * 512, (mt + 1) * 512)
                for fc in range(NF):
                    col = 0
                    for (s_, a_, w_) in segs:
                        p.bump("sync", ld3, 16, lambda e, fc=fc, s_=s_, a_=a_, w_=w_, col=col, grhs=grhs: e.dma_start(
                            out=grhs[:, fc * 512 + col: fc * 512 + col + w_],
                            in_=ccg_o[s_ * F + fc * 128: s_ * F + (fc + 1) * 128, a_:a_ + w_]))
                        p.bump("sync", ld3, 16, lambda e, fc=fc, s_=s_, a_=a_, w_=w_, col=col, urhs=urhs: e.dma_start(
                            out=urhs[:, fc * 512 + col: fc * 512 + col + w_],
                            in_=ccu_o[s_ * F + fc * 128: s_ * F + (fc + 1) * 128, a_:a_ + w_]))
                        col += w_
                p.wait("tensor", ld2)
                p.wait("tensor", ld3)
                for ss in range(4):
                    sidx = h6 * 4 + ss
                    gb = pb[ss % 2]
                    ub = pb[2 + ss % 2]
                    p.wait("tensor", sc1, sc1.n)
                    for ci in range(NF):
                        p.bump("tensor", mm1, 1, lambda e, gb=gb, ci=ci, ss=ss, qxf=qxf, grhs=grhs: e.matmul(
                            gb[:], qxf[:, ci * 512 + ss * 128: ci * 512 + ss * 128 + 128],
                            grhs[:, ci * 512:(ci + 1) * 512],
                            start=(ci == 0), stop=(ci == NF - 1)))
                    for ci in range(NF):
                        p.bump("tensor", mm1, 1, lambda e, ub=ub, ci=ci, ss=ss, qxf=qxf, urhs=urhs: e.matmul(
                            ub[:], qxf[:, ci * 512 + ss * 128: ci * 512 + ss * 128 + 128],
                            urhs[:, ci * 512:(ci + 1) * 512],
                            start=(ci == 0), stop=(ci == NF - 1)))
                    p.wait("scalar", mm1)
                    p.wait("scalar", vc1, vc1.n)
                    p.bump("scalar", sc1, 1, lambda e, gb=gb, sidx=sidx, gts=gts: e.activation(
                        gts[:], gb[:], AF.Silu, scale=sx_all[:, sidx:sidx + 1]))
                    p.bump("scalar", sc1, 1, lambda e, ub=ub, sidx=sidx, uts=uts: e.activation(
                        uts[:], ub[:], AF.Identity, scale=sx_all[:, sidx:sidx + 1]))
                    p.wait("vector", sc1)
                    p.bump("vector", vc1, 1, lambda e, ss=ss, mt=mt, tsb=tsb, gts=gts, uts=uts: e.tensor_tensor(
                        tsb[:, ss * I + mt * 512: ss * I + mt * 512 + 512], gts[:], uts[:], ALU.mult))
                    p.vdrain()
            p.wait("vector", out0, out0.n)
            p.wait("scalar", out0, out0.n)
            for ss in range(4):
                sidx = h6 * 4 + ss
                emit_rowscale(128, tsb[:, ss * I:(ss + 1) * I], save_scale=st_all[:, sidx:sidx + 1])
                p.wait("vector", st1, st1.n)
                for w0 in range(0, I, 2048):
                    w1 = min(w0 + 2048, I)
                    emit_quant(128, w1 - w0, tsb[:, ss * I + w0: ss * I + w1], qt16[:, w0:w1])
                p.wait("gpsimd", vc0)
                p.bump("gpsimd", st1, 16, lambda e, h6=h6, ss=ss, qt16=qt16: e.dma_start(
                    out=qtd[h6 * 512 + ss * 128: h6 * 512 + ss * 128 + 128, :], in_=qt16[:]))

        if upto == 6 and h6 == 0:
            return _trunc(st1, st1.n)

        with nc.sbuf_tensor(f"qtf{h6}", [128, NMC * 512], BF16) as qtf, \
             nc.sbuf_tensor(f"dnr{h6}", [128, NMC * 512], BF16) as dnr, \
             nc.sbuf_tensor(f"y2f{h6}", [128, 4 * F], BF16) as y2f:
            ytmp = qs3
            yout = qs4

            p.wait("sync", cc0, 4)
            p.wait("sync", st1)
            for mc in range(NMC):
                p.bump("sync", ld2, 16, lambda e, mc=mc, h6=h6, qtf=qtf: e.dma_start(
                    out=qtf[:, mc * 512:(mc + 1) * 512],
                    in_=qtd[h6 * 512:(h6 + 1) * 512, mc * 128:(mc + 1) * 128],
                    transpose=True))
            for jt in range(4):
                p.wait("sync", mm2, mm2.n)
                for mc in range(NMC):
                    for sj in range(2):
                        s_ = 2 * jt + sj
                        p.bump("sync", ld3, 16, lambda e, mc=mc, s_=s_, sj=sj, dnr=dnr: e.dma_start(
                            out=dnr[:, mc * 512 + sj * 256: mc * 512 + sj * 256 + 256],
                            in_=ccd_o[s_ * I + mc * 128: s_ * I + (mc + 1) * 128, :]))
                p.wait("tensor", ld2)
                p.wait("tensor", ld3)
                for ss in range(4):
                    sidx = h6 * 4 + ss
                    db = pb[4 + ss % 2]
                    p.wait("tensor", sc1, sc1.n)
                    for mc in range(NMC):
                        p.bump("tensor", mm2, 1, lambda e, db=db, mc=mc, ss=ss, qtf=qtf, dnr=dnr: e.matmul(
                            db[:], qtf[:, mc * 512 + ss * 128: mc * 512 + ss * 128 + 128],
                            dnr[:, mc * 512:(mc + 1) * 512],
                            start=(mc == 0), stop=(mc == NMC - 1)))
                    p.wait("scalar", mm2)
                    p.wait("scalar", vc1, vc1.n)
                    p.bump("scalar", sc1, 1, lambda e, db=db, ss=ss, jt=jt, sidx=sidx, y2f=y2f: e.activation(
                        y2f[:, ss * F + jt * 512: ss * F + jt * 512 + 512], db[:], AF.Identity,
                        scale=st_all[:, sidx:sidx + 1]))
            for ss in range(4):
                sidx = h6 * 4 + ss
                srow = h6 * 512 + ss * 128
                p.wait("vector", sc1, sc1.n)
                p.wait("vector", sc0, sc0.n)
                for j4 in range(4):
                    p.bump("vector", vc1, 1, lambda e, ss=ss, j4=j4, sidx=sidx, y2f=y2f: e.scalar_tensor_tensor(
                        ytmp[:, j4 * 512:(j4 + 1) * 512],
                        y2f[:, ss * F + j4 * 512: ss * F + j4 * 512 + 512], 1.0,
                        x1s[:, sidx * F + j4 * 512: sidx * F + j4 * 512 + 512],
                        ALU.mult, ALU.add, accum_out=lnsc[:, j4:j4 + 1]))
                    p.vdrain()
                p.wait("scalar", out0, out0.n)
                emit_ln(ytmp[:], yout[:])
                p.wait("gpsimd", sc0)
                p.bump("gpsimd", out0, 16, lambda e, srow=srow: e.dma_start(
                    out=y[srow:srow + 128, :], in_=yout[:]))

    p.wait("gpsimd", out0)

    return _finish()


_NC_CACHE = {}


def kernel(**inputs):
    x = np.ascontiguousarray(np.asarray(inputs["L0_attn_x"], np.float32))
    Wq = np.ascontiguousarray(np.asarray(inputs["L0_attn_Wq"], np.float32))
    Wk = np.ascontiguousarray(np.asarray(inputs["L0_attn_Wk"], np.float32))
    Wv = np.ascontiguousarray(np.asarray(inputs["L0_attn_Wv"], np.float32))
    Wo = np.ascontiguousarray(np.asarray(inputs["L0_attn_Wo"], np.float32))
    bq = np.asarray(inputs["L0_attn_bq"], np.float32)
    bk = np.asarray(inputs["L0_attn_bk"], np.float32)
    bv = np.asarray(inputs["L0_attn_bv"], np.float32)
    bo_ = np.asarray(inputs["L0_attn_bo"], np.float32)
    wg_ = np.ascontiguousarray(np.asarray(inputs["L0_ffn_w_gate"], np.float32))
    wu_ = np.ascontiguousarray(np.asarray(inputs["L0_ffn_w_up"], np.float32))
    wd_ = np.ascontiguousarray(np.asarray(inputs["L0_ffn_w_down"], np.float32))
    didx = np.asarray(inputs["L0_ffn_down_reorder_index"]).astype(np.int32).ravel()

    if "nc" not in _NC_CACHE:
        _NC_CACHE["nc"] = build_program()
    nc = _NC_CACHE["nc"]

    in_maps = []
    for c in range(8):
        b_ = c // 4
        g_ = c % 4
        cols = slice(g_ * HG, (g_ + 1) * HG)
        dslice = didx[c * GUP:(c + 1) * GUP]
        dpad = np.zeros(768, np.int32)
        dpad[:GUP] = dslice
        didx_l = np.ascontiguousarray(dpad.reshape(6, 128).T)
        bq_l = np.ascontiguousarray(bq[cols].reshape(4, 128).T)
        bk_l = np.ascontiguousarray(bk[cols].reshape(4, 128).T)
        bv_l = np.ascontiguousarray(bv[cols].reshape(4, 128).T)
        bqkv_l = np.concatenate([bq_l, bk_l, bv_l], axis=1)
        in_maps.append({
            "xb": x[b_],
            "x_tok": np.ascontiguousarray(x[b_, g_ * TOK:(g_ + 1) * TOK]),
            "wq": np.ascontiguousarray(Wq[:, cols]),
            "wk": np.ascontiguousarray(Wk[:, cols]),
            "wv": np.ascontiguousarray(Wv[:, cols]),
            "wo": Wo,
            "bqkv": np.ascontiguousarray(bqkv_l),
            "bo": np.ascontiguousarray(bo_.reshape(1, F)),
            "wg": wg_,
            "wu": wu_,
            "wd_my": np.ascontiguousarray(wd_[c * DWN:(c + 1) * DWN]),
            "didx_my": didx_l,
            "y": None,
        })
    for m in in_maps:
        del m["y"]

    res = run_bass_kernel_spmd(nc, in_maps, core_ids=list(range(8)))
    out = np.empty((B, S, F), np.float32)
    for c in range(8):
        b_ = c // 4
        g_ = c % 4
        out[b_, g_ * TOK:(g_ + 1) * TOK] = res.results[c]["y"]
    return out


# version 09 (final)
# speedup vs baseline: 1.0288x; 1.0288x over previous
import numpy as np
import ml_dtypes
from contextlib import ExitStack
import concourse.bass as bass
import concourse.mybir as mybir
from concourse.bass import IndirectOffsetOnAxis
from concourse.bass_utils import run_bass_kernel_spmd

F32 = mybir.dt.float32
BF16 = mybir.dt.bfloat16
I32 = mybir.dt.int32
AF = mybir.ActivationFunctionType
ALU = mybir.AluOpType
AX = mybir.AxisListType.X

B, S, F, HG, I = 2, 4096, 2048, 512, 5632
TOK = 1024
NF = F // 128
NSC = S // 512
NT = S // 128
GUP = I // 8
DWN = F // 8
NMT = I // 512
NMC = I // 128

LN2C = float(np.log(2.0))
INV_LN2 = float(1.0 / np.log(2.0))
INV_SCALE = float(np.float32(1.0 / np.float64(np.sqrt(32.0))))
EXP_SHIFT = -8.0
RG8 = [[0, 1, 2, 3, 4, 5, 6, 7]]
RG4 = [[0, 1, 2, 3], [4, 5, 6, 7]]

ENGS = ("sync", "tensor", "vector", "scalar", "gpsimd")


class Ctr:
    __slots__ = ("sem", "n", "name")
    def __init__(self, sem, name):
        self.sem = sem
        self.n = 0
        self.name = name


class Prog:
    def __init__(self):
        self.streams = {e: [] for e in ENGS}
        self.meta = {e: [] for e in ENGS}
        self.holder = {}
        self.maxwait = {}

    def em(self, eng, fn, meta=("op",)):
        self.streams[eng].append(fn)
        self.meta[eng].append(meta)

    def wait(self, eng, ctr, val=None):
        v = ctr.n if val is None else val
        if v <= 0:
            return
        key = (eng, ctr.name)
        if v <= self.maxwait.get(key, 0):
            return
        self.maxwait[key] = v
        self.em(eng, lambda e, s=ctr.sem, v=v: e.wait_ge(s, v), ("wait", ctr.name, v))

    def bump(self, eng, ctr, k, fn):
        ctr.n += k
        self.em(eng, lambda e, fn=fn, s=ctr.sem, k=k: fn(e).then_inc(s, k),
                ("bump", ctr.name, k))

    def run(self, eng, fn):
        self.em(eng, fn)

    def vdrain(self):
        self.em("vector", lambda e: e.drain())

    def check(self):
        pos = {e: 0 for e in ENGS}
        cnt = {}
        progressed = True
        while progressed:
            progressed = False
            for e in ENGS:
                while pos[e] < len(self.meta[e]):
                    m = self.meta[e][pos[e]]
                    if m[0] == "wait" and cnt.get(m[1], 0) < m[2]:
                        break
                    if m[0] == "bump":
                        cnt[m[1]] = cnt.get(m[1], 0) + m[2]
                    pos[e] += 1
                    progressed = True
        stuck = {e: (pos[e], len(self.meta[e])) for e in ENGS if pos[e] < len(self.meta[e])}
        if stuck:
            detail = {e: self.meta[e][pos[e]] for e in stuck}
            raise RuntimeError(
                f"static deadlock: stuck={stuck} blockers={detail} counts={cnt}")


def build_program(upto=None):
    nc = bass.Bass(trn_type="TRN2", target_bir_lowering=False, disable_frame_to_traceback=True)

    xb = nc.dram_tensor("xb", [S, F], F32, kind="ExternalInput").ap()
    x_tok = nc.dram_tensor("x_tok", [TOK, F], F32, kind="ExternalInput").ap()
    wq = nc.dram_tensor("wq", [F, HG], F32, kind="ExternalInput").ap()
    wk = nc.dram_tensor("wk", [F, HG], F32, kind="ExternalInput").ap()
    wv = nc.dram_tensor("wv", [F, HG], F32, kind="ExternalInput").ap()
    wo = nc.dram_tensor("wo", [F, F], F32, kind="ExternalInput").ap()
    bqkv = nc.dram_tensor("bqkv", [128, 12], F32, kind="ExternalInput").ap()
    bo = nc.dram_tensor("bo", [1, F], F32, kind="ExternalInput").ap()
    wg = nc.dram_tensor("wg", [I, F], F32, kind="ExternalInput").ap()
    wu = nc.dram_tensor("wu", [I, F], F32, kind="ExternalInput").ap()
    wd_my = nc.dram_tensor("wd_my", [DWN, I], F32, kind="ExternalInput").ap()
    didx_my = nc.dram_tensor("didx_my", [128, 6], I32, kind="ExternalInput").ap()
    y = nc.dram_tensor("y", [TOK, F], F32, kind="ExternalOutput").ap()

    xb16 = nc.dram_tensor("xb16", [S, F], BF16).ap()
    vfm = nc.dram_tensor("vfm", [HG, S], BF16).ap()
    gtmp = nc.dram_tensor("gtmp", [GUP, F], BF16).ap()
    utmp = nc.dram_tensor("utmp", [GUP, F], BF16).ap()
    dtmp = nc.dram_tensor("dtmp", [DWN, I], BF16).ap()
    ccg_i = nc.dram_tensor("ccg_i", [F, GUP], BF16).ap()
    ccg_o = nc.dram_tensor("ccg_o", [8 * F, GUP], BF16, addr_space="Shared").ap()
    ccu_i = nc.dram_tensor("ccu_i", [F, GUP], BF16).ap()
    ccu_o = nc.dram_tensor("ccu_o", [8 * F, GUP], BF16, addr_space="Shared").ap()
    ccd_i = nc.dram_tensor("ccd_i", [I, DWN], BF16).ap()
    ccd_o = nc.dram_tensor("ccd_o", [8 * I, DWN], BF16, addr_space="Shared").ap()
    cc2i = nc.dram_tensor("cc2i", [HG, S], BF16).ap()
    cc2o = nc.dram_tensor("cc2o", [F, S], BF16).ap()
    qxd = nc.dram_tensor("qxd", [TOK, F], BF16).ap()
    qtd = nc.dram_tensor("qtd", [TOK, I], BF16).ap()

    p = Prog()
    es = ExitStack()
    sb = lambda *a: es.enter_context(nc.sbuf_tensor(*a))
    psb = lambda *a: es.enter_context(nc.psum_tensor(*a))
    mksem = lambda n: Ctr(es.enter_context(nc.semaphore(n)), n)

    didx_sb = sb("didx_sb", [128, 6], I32)
    bqkv_sb = sb("bqkv_sb", [128, 12], F32)
    bo16 = sb("bo16", [1, F], BF16)
    bo32 = sb("bo32", [1, F], F32)
    c448 = sb("c448", [128, 1], F32)
    bm8 = sb("bm8", [128, 1], F32)
    b9l2 = sb("b9l2", [128, 1], F32)
    bm9l2 = sb("bm9l2", [128, 1], F32)
    ones_col = sb("ones_col", [128, 1], BF16)
    ones_row = sb("ones_row", [1, 128], BF16)
    e2b = sb("e2b", [128, 256], F32)
    qs0 = sb("qs0", [128, 2048], F32)
    qs1 = sb("qs1", [128, 2048], F32)
    qs2 = sb("qs2", [128, 2048], F32)
    qs3 = sb("qs3", [128, 2048], F32)
    qs4 = sb("qs4", [128, 2048], F32)
    qi0 = sb("qi0", [128, 2048], I32)
    rmax = sb("rmax", [128, 1], F32)
    rmax2 = sb("rmax2", [128, 1], F32)
    s448 = sb("s448", [128, 1], F32)
    wsc = sb("wsc", [128, 1], F32)
    lnsc = sb("lnsc", [128, 12], F32)
    sx_all = sb("sx_all", [128, 8], F32)
    st_all = sb("st_all", [128, 8], F32)

    pb = [psb(f"pb{i}", [128, 512], F32) for i in range(8)]

    ld0 = mksem("ld0"); ld1 = mksem("ld1"); ld2 = mksem("ld2"); ld3 = mksem("ld3")
    st0 = mksem("st0"); st1 = mksem("st1")
    vc0 = mksem("vc0"); vc1 = mksem("vc1"); vc2 = mksem("vc2")
    sc0 = mksem("sc0"); sc1 = mksem("sc1")
    mm0 = mksem("mm0"); mm1 = mksem("mm1"); mm2 = mksem("mm2")
    cc0 = mksem("cc0"); gg0 = mksem("gg0"); out0 = mksem("out0"); fq0 = mksem("fq0")

    def _finish():
        p.check()
        with nc.Block() as block:
            @block.sync
            def _(sync):
                for fn in p.streams["sync"]:
                    fn(sync)

            @block.tensor
            def _(tensor):
                for fn in p.streams["tensor"]:
                    fn(tensor)

            @block.vector
            def _(vector):
                for fn in p.streams["vector"]:
                    fn(vector)

            @block.scalar
            def _(scalar):
                for fn in p.streams["scalar"]:
                    fn(scalar)

            @block.gpsimd
            def _(gpsimd):
                for fn in p.streams["gpsimd"]:
                    fn(gpsimd)

        if not p.holder.get("_skip_es_close"):
            es.close()
        return nc

    def _trunc(ctr, val):
        p.wait("gpsimd", ctr, val)
        p.bump("gpsimd", out0, 16, lambda e: e.dma_start(out=y[0:128, :], in_=qs0[:]))
        p.wait("gpsimd", out0, out0.n)
        p.holder["_skip_es_close"] = True
        return _finish()

    p.bump("sync", ld0, 16, lambda e: e.dma_start(out=didx_sb[:], in_=didx_my))
    p.bump("sync", ld0, 16, lambda e: e.dma_start(out=bqkv_sb[:], in_=bqkv))
    p.bump("sync", ld0, 16, lambda e: e.dma_start(out=bo32[:], in_=bo))
    for fn in (
        lambda e: e.memset(c448[:], 448.0),
        lambda e: e.memset(bm8[:], EXP_SHIFT),
        lambda e: e.memset(b9l2[:], 9.0 * LN2C),
        lambda e: e.memset(bm9l2[:], -9.0 * LN2C),
        lambda e: e.memset(ones_col[:], 1.0),
        lambda e: e.memset(ones_row[:], 1.0),
        lambda e: e.memset(e2b[0:1, 0:64], 1.0),
        lambda e: e.memset(e2b[0:1, 64:128], 0.0),
        lambda e: e.memset(e2b[0:1, 128:192], 0.0),
        lambda e: e.memset(e2b[0:1, 192:256], 1.0),
    ):
        p.run("vector", fn)
    p.wait("vector", ld0)
    p.bump("vector", vc0, 1, lambda e: e.tensor_copy(bo16[:], bo32[:]))
    p.vdrain()

    def emit_rowscale(rows, src_ap, save_scale=None):
        p.wait("vector", sc0, sc0.n)
        p.run("vector", lambda e, s=src_ap: e.tensor_reduce(
            rmax[0:rows, :], s, AX, ALU.max, apply_absolute_value=True))
        p.vdrain()
        p.run("vector", lambda e: e.reciprocal(s448[0:rows, :], rmax[0:rows, :]))
        p.vdrain()
        p.run("vector", lambda e: e.tensor_scalar_mul(s448[0:rows, :], s448[0:rows, :], 448.0))
        p.vdrain()
        if save_scale is not None:
            p.run("vector", lambda e, d=save_scale: e.tensor_scalar_mul(
                d, rmax[0:rows, :], 1.0 / 448.0))
            p.vdrain()
        p.bump("vector", vc0, 1, lambda e: e.engine_nop())

    def emit_quant(rows, width, src_ap, dst_ap):
        a0 = qs0[0:rows, 0:width]
        a1 = qs1[0:rows, 0:width]
        a2 = qs2[0:rows, 0:width]
        a4 = qs4[0:rows, 0:width]
        ii = qi0[0:rows, 0:width]
        sdr = lambda: p.em("scalar", lambda e: e.drain())
        p.wait("scalar", vc0)
        p.bump("scalar", sc0, 1, lambda e, s=src_ap: e.activation(
            a0, s, AF.Identity, scale=s448[0:rows, :]))
        sdr()
        p.bump("scalar", sc0, 1, lambda e: e.activation(a1, a0, AF.Abs))
        p.wait("vector", sc0)
        p.bump("vector", vc0, 1, lambda e: e.tensor_scalar_max(a2, a1, 2.0 ** -6))
        p.vdrain()
        p.wait("scalar", vc0)
        p.bump("scalar", sc0, 1, lambda e: e.activation(a1, a2, AF.Ln))
        p.wait("vector", sc0)
        p.run("vector", lambda e: e.tensor_scalar(a2, a1, INV_LN2, 5.5, ALU.mult, ALU.add))
        p.vdrain()
        p.run("vector", lambda e: e.tensor_copy(ii, a2))
        p.vdrain()
        p.bump("vector", vc0, 1, lambda e: e.tensor_copy(a2, ii))
        p.vdrain()
        p.wait("scalar", vc0)
        p.bump("scalar", sc0, 1, lambda e: e.activation(
            a1, a2, AF.Exp, bias=b9l2[0:rows, :], scale=-LN2C))
        sdr()
        p.bump("scalar", sc0, 1, lambda e: e.activation(
            a4, a2, AF.Exp, bias=bm9l2[0:rows, :], scale=LN2C))
        p.wait("vector", sc0)
        p.run("vector", lambda e: e.tensor_mul(a2, a0, a1))
        p.vdrain()
        p.run("vector", lambda e: e.tensor_copy(ii, a2))
        p.vdrain()
        p.run("vector", lambda e: e.tensor_copy(a2, ii))
        p.vdrain()
        p.bump("vector", vc0, 1, lambda e, d=dst_ap: e.tensor_mul(d, a2, a4))
        p.vdrain()

    def emit_ln(src_ap, dst_ap):
        p.run("vector", lambda e, s=src_ap: e.scalar_tensor_tensor(
            qs1[:, 0:F], s, 1.0, s, ALU.mult, ALU.mult, accum_out=lnsc[:, 4:5]))
        p.vdrain()
        p.run("vector", lambda e: e.tensor_add(lnsc[:, 5:6], lnsc[:, 0:1], lnsc[:, 1:2]))
        p.vdrain()
        p.run("vector", lambda e: e.tensor_add(lnsc[:, 6:7], lnsc[:, 2:3], lnsc[:, 3:4]))
        p.vdrain()
        p.run("vector", lambda e: e.tensor_add(lnsc[:, 7:8], lnsc[:, 5:6], lnsc[:, 6:7]))
        p.vdrain()
        p.run("vector", lambda e: e.tensor_scalar_mul(lnsc[:, 8:9], lnsc[:, 7:8], 1.0 / F))
        p.vdrain()
        p.run("vector", lambda e: e.tensor_scalar_mul(lnsc[:, 9:10], lnsc[:, 4:5], 1.0 / F))
        p.vdrain()
        p.run("vector", lambda e: e.tensor_tensor(lnsc[:, 10:11], lnsc[:, 8:9], lnsc[:, 8:9], ALU.mult))
        p.vdrain()
        p.run("vector", lambda e: e.tensor_sub(lnsc[:, 9:10], lnsc[:, 9:10], lnsc[:, 10:11]))
        p.vdrain()
        p.bump("vector", vc0, 1, lambda e: e.tensor_scalar_add(lnsc[:, 9:10], lnsc[:, 9:10], 1e-5))
        p.vdrain()
        p.wait("scalar", vc0)
        p.bump("scalar", sc0, 1, lambda e: e.activation(lnsc[:, 10:11], lnsc[:, 9:10], AF.Ln))
        p.wait("vector", sc0)
        p.bump("vector", vc0, 1, lambda e: e.engine_nop())
        p.wait("scalar", vc0)
        p.bump("scalar", sc0, 1, lambda e: e.activation(
            lnsc[:, 11:12], lnsc[:, 10:11], AF.Exp, scale=-0.5))
        p.wait("vector", sc0)
        p.bump("vector", vc0, 1, lambda e: e.scalar_tensor_tensor(
            lnsc[:, 10:11], lnsc[:, 8:9], -1.0, lnsc[:, 11:12], ALU.mult, ALU.mult))
        p.vdrain()
        p.wait("scalar", vc0)
        p.bump("scalar", sc0, 1, lambda e, s=src_ap, d=dst_ap: e.activation(
            d, s, AF.Identity, bias=lnsc[:, 10:11], scale=lnsc[:, 11:12]))

    def emit_p1_chunk(src_dram, tmp_dram, j, wrX, qrX):
        rows = 128 if j < 5 else 64
        r0 = j * 128
        p.wait("gpsimd", ld0)
        p.wait("gpsimd", vc0, vc0.n)
        p.bump("gpsimd", gg0, 16, lambda e, rows=rows, j=j, src=src_dram, wrX=wrX: e.indirect_dma_start(
            wrX[0:rows, 0:F], None, src,
            IndirectOffsetOnAxis(ap=didx_sb[0:rows, j:j + 1], axis=0)))
        p.wait("vector", gg0)
        emit_rowscale(rows, wrX[0:rows, 0:F], save_scale=wsc[0:rows, :])
        emit_quant(rows, F, wrX[0:rows, 0:F], qs3[0:rows, 0:F])
        p.wait("vector", st1, st1.n)
        p.bump("vector", fq0, 1, lambda e, rows=rows, qrX=qrX: e.tensor_scalar(
            qrX[0:rows, 0:F], qs3[0:rows, 0:F], wsc[0:rows, 0:1], None, ALU.mult))
        p.vdrain()
        p.wait("gpsimd", fq0)
        p.bump("gpsimd", st1, 16, lambda e, rows=rows, r0=r0, t=tmp_dram, qrX=qrX: e.dma_start(
            out=t[r0:r0 + rows, :], in_=qrX[0:rows, 0:F]))

    WDC = [(0, 2048), (2048, 4096), (4096, I)]

    def emit_wd(j, wrX, qrX):
        p.wait("sync", ld1, ld1.n)
        for c, (w0, w1) in enumerate(WDC):
            p.wait("sync", vc0, vc0.n)
            p.bump("sync", ld1, 16, lambda e, j=j, w0=w0, w1=w1, wrX=wrX: e.dma_start(
                out=wrX[:, 0:w1 - w0], in_=wd_my[j * 128:(j + 1) * 128, w0:w1]))
            p.wait("vector", ld1)
            if c == 0:
                p.wait("vector", sc0, sc0.n)
                p.bump("vector", vc0, 1, lambda e, w0=w0, w1=w1, wrX=wrX: e.tensor_reduce(
                    rmax[:, :], wrX[:, 0:w1 - w0], AX, ALU.max, apply_absolute_value=True))
            else:
                p.run("vector", lambda e, w0=w0, w1=w1, wrX=wrX: e.tensor_reduce(
                    rmax2[:, :], wrX[:, 0:w1 - w0], AX, ALU.max, apply_absolute_value=True))
                p.vdrain()
                p.bump("vector", vc0, 1, lambda e: e.tensor_tensor(
                    rmax[:, :], rmax[:, :], rmax2[:, :], ALU.max))
            p.vdrain()
        p.run("vector", lambda e: e.reciprocal(s448[:, :], rmax[:, :]))
        p.vdrain()
        p.run("vector", lambda e: e.tensor_scalar_mul(s448[:, :], s448[:, :], 448.0))
        p.vdrain()
        p.run("vector", lambda e: e.tensor_scalar_mul(wsc[:, :], rmax[:, :], 1.0 / 448.0))
        p.vdrain()
        p.bump("vector", vc0, 1, lambda e: e.engine_nop())
        for c, (w0, w1) in enumerate(WDC):
            p.wait("sync", vc0, vc0.n)
            p.bump("sync", ld1, 16, lambda e, j=j, w0=w0, w1=w1, wrX=wrX: e.dma_start(
                out=wrX[:, 0:w1 - w0], in_=wd_my[j * 128:(j + 1) * 128, w0:w1]))
            p.wait("vector", ld1)
            p.bump("vector", vc0, 1, lambda e: e.engine_nop())
            emit_quant(128, w1 - w0, wrX[:, 0:w1 - w0], qs3[:, 0:w1 - w0])
            p.wait("vector", st1, st1.n)
            p.bump("vector", fq0, 1, lambda e, w0=w0, w1=w1, qrX=qrX: e.tensor_scalar(
                qrX[:, 0:w1 - w0], qs3[:, 0:w1 - w0], wsc[:, 0:1], None, ALU.mult))
            p.vdrain()
            p.wait("gpsimd", fq0)
            p.bump("gpsimd", st1, 16, lambda e, j=j, w0=w0, w1=w1, qrX=qrX: e.dma_start(
                out=dtmp[j * 128:(j + 1) * 128, w0:w1], in_=qrX[:, 0:w1 - w0]))

    def emit_fc_part(tmp_dram, cci, fcs, ttX):
        for fc in fcs:
            p.wait("sync", st1)
            p.wait("sync", st0, st0.n)
            p.bump("sync", ld1, 16, lambda e, fc=fc, t=tmp_dram, ttX=ttX: e.dma_start(
                out=ttX[:, 0:GUP], in_=t[:, fc * 128:(fc + 1) * 128], transpose=True))
            p.wait("gpsimd", ld1)
            p.bump("gpsimd", st0, 16, lambda e, fc=fc, cci=cci, ttX=ttX: e.dma_start(
                out=cci[fc * 128:(fc + 1) * 128, :], in_=ttX[:, 0:GUP]))

    with nc.sbuf_tensor("xc32", [128, F], F32) as xc32, \
         nc.sbuf_tensor("xc16", [128, F], BF16) as xc16:
        p.wait("sync", ld1)
        p.wait("sync", st0, st0.n)
        p.wait("sync", st1)
        for ch in range(S // 128):
            p.wait("sync", vc1, vc1.n)
            p.bump("sync", ld1, 16, lambda e, ch=ch: e.dma_start(
                out=xc32[:], in_=xb[ch * 128:(ch + 1) * 128, :]))
            p.wait("vector", ld1)
            p.wait("vector", st1, st1.n)
            p.bump("vector", vc1, 1, lambda e: e.tensor_copy(xc16[:], xc32[:]))
            p.vdrain()
            p.wait("gpsimd", vc1)
            p.bump("gpsimd", st1, 16, lambda e, ch=ch: e.dma_start(
                out=xb16[ch * 128:(ch + 1) * 128, :], in_=xc16[:]))
    if upto == 2:
        return _trunc(st1, st1.n)

    with nc.sbuf_tensor("qfm", [128, 4 * S], BF16) as qfm, \
         nc.sbuf_tensor("kfm", [128, 4 * S], BF16) as kfm:

        with nc.sbuf_tensor("w3f", [128, HG], F32) as w3f, \
             nc.sbuf_tensor("wq16", [128, NF * 512], BF16) as wq16, \
             nc.sbuf_tensor("wk16", [128, NF * 512], BF16) as wk16, \
             nc.sbuf_tensor("wv16", [128, NF * 512], BF16) as wv16, \
             nc.sbuf_tensor("xfm", [128, NF * 512], BF16) as xfm, \
             nc.sbuf_tensor("vtile", [128, 512], BF16) as vtile:

            p.wait("sync", st1)
            for wsrc, wdst in ((wq, wq16), (wk, wk16), (wv, wv16)):
                for ci in range(NF):
                    p.wait("sync", vc1, vc1.n)
                    p.bump("sync", ld1, 16, lambda e, ci=ci, wsrc=wsrc: e.dma_start(
                        out=w3f[:], in_=wsrc[ci * 128:(ci + 1) * 128, :]))
                    p.wait("vector", ld1)
                    p.bump("vector", vc1, 1, lambda e, ci=ci, wdst=wdst: e.tensor_copy(
                        wdst[:, ci * 512:(ci + 1) * 512], w3f[:]))
                    p.vdrain()

            for sc in range(NSC):
                p.wait("sync", mm0, mm0.n)
                p.wait("sync", vc1, vc1.n)
                for fc in range(NF):
                    p.bump("sync", ld2, 16, lambda e, fc=fc, sc=sc: e.dma_start(
                        out=xfm[:, fc * 512:(fc + 1) * 512],
                        in_=xb16[sc * 512:(sc + 1) * 512, fc * 128:(fc + 1) * 128],
                        transpose=True))
                p.wait("tensor", ld2)
                p.wait("tensor", vc1, vc1.n)
                for pi, wtile in enumerate((wq16, wk16, wv16)):
                    for j in range(4):
                        psx = pb[(pi * 4 + j) % 3]
                        p.wait("tensor", sc1, sc1.n)
                        for ci in range(NF):
                            p.bump("tensor", mm0, 1, lambda e, psx=psx, ci=ci, j=j, wtile=wtile: e.matmul(
                                psx[:], wtile[:, ci * 512 + j * 128: ci * 512 + j * 128 + 128],
                                xfm[:, ci * 512:(ci + 1) * 512],
                                start=(ci == 0), stop=(ci == NF - 1)))
                        p.wait("scalar", mm0)
                        bia = bqkv_sb[:, pi * 4 + j: pi * 4 + j + 1]
                        if pi == 0:
                            p.bump("scalar", sc1, 1, lambda e, psx=psx, j=j, sc=sc, bia=bia: e.activation(
                                qfm[:, j * S + sc * 512: j * S + sc * 512 + 512], psx[:], AF.Identity, bias=bia))
                        elif pi == 1:
                            p.bump("scalar", sc1, 1, lambda e, psx=psx, j=j, sc=sc, bia=bia: e.activation(
                                kfm[:, j * S + sc * 512: j * S + sc * 512 + 512], psx[:], AF.Identity, bias=bia))
                        else:
                            p.wait("scalar", st1, st1.n)
                            p.bump("scalar", sc1, 1, lambda e, psx=psx, bia=bia: e.activation(
                                vtile[:], psx[:], AF.Identity, bias=bia))
                            p.wait("gpsimd", sc1)
                            p.bump("gpsimd", st1, 16, lambda e, j=j, sc=sc: e.dma_start(
                                out=vfm[j * 128:(j + 1) * 128, sc * 512:(sc + 1) * 512],
                                in_=vtile[:]))
        if upto == 3:
            return _trunc(st1, st1.n)

        with nc.sbuf_tensor("vtm", [128, NT * 512], BF16) as vtm, \
             nc.sbuf_tensor("pt", [128, 4 * 512], BF16) as pt, \
             nc.sbuf_tensor("rd4", [128, 512], F32) as rd4, \
             nc.sbuf_tensor("rdz", [1, 4 * 512], F32) as rdz, \
             nc.sbuf_tensor("bc2", [128, 1024], F32) as bc2, \
             nc.sbuf_tensor("fn0", [128, 512], BF16) as fn0, \
             nc.sbuf_tensor("wrows2", [128, F], F32) as wrows2, \
             nc.sbuf_tensor("qrows2", [128, F], BF16) as qrows2, \
             nc.sbuf_tensor("ttile2", [128, GUP], BF16) as ttile2:

            p.wait("sync", st1)
            for tt in range(NT):
                p.bump("sync", ld3, 16, lambda e, tt=tt: e.dma_start(
                    out=vtm[:, tt * 512:(tt + 1) * 512],
                    in_=vfm[:, tt * 128:(tt + 1) * 128],
                    transpose=True))
            p.wait("tensor", ld3)
            p.wait("tensor", sc1, sc1.n)
            p.wait("tensor", vc0, vc0.n)

            wg_fc_parts = [[0, 1, 2], [3, 4, 5], [6, 7, 8], [9, 10, 11], [12, 13], [14, 15]]
            exp_after = {}
            for sc in range(NSC):
                for qd in range(2):
                    k = sc * 2 + qd
                    if k < 6:
                        emit_p1_chunk(wg, gtmp, k, wrows2, qrows2)
                    elif k < 12:
                        emit_p1_chunk(wu, utmp, k - 6, wrows2, qrows2)
                        emit_fc_part(gtmp, ccg_i, wg_fc_parts[k - 6], ttile2)
                    elif k == 12:
                        emit_wd(0, wrows2, qrows2)
                    elif k == 13:
                        emit_wd(1, wrows2, qrows2)
                    elif k == 14:
                        emit_fc_part(utmp, ccu_i, list(range(NF)), ttile2)
                    av_free = vc2.n
                    for tt in range(NT):
                        for pr in range(2):
                            cp = 2 * qd + pr
                            sA = pb[2 * pr]
                            sB = pb[2 * pr + 1]
                            if pr in exp_after:
                                p.wait("tensor", sc1, exp_after[pr])
                            kbase = cp * S + tt * 128
                            qbase = cp * S + sc * 512
                            p.bump("tensor", mm0, 1, lambda e, sA=sA, kbase=kbase, qbase=qbase: e.matmul(
                                sA[:], kfm[0:64, kbase:kbase + 128], qfm[0:64, qbase:qbase + 512],
                                start=True, stop=True, tile_position=(0, 0)))
                            p.bump("tensor", mm0, 1, lambda e, sB=sB, kbase=kbase, qbase=qbase: e.matmul(
                                sB[:], kfm[64:128, kbase:kbase + 128], qfm[64:128, qbase:qbase + 512],
                                start=True, stop=True, tile_position=(64, 0)))
                            p.wait("scalar", mm0)
                            p.wait("scalar", mm1, mm1.n)
                            p.wait("scalar", mm2, mm2.n)
                            p.bump("scalar", sc1, 1, lambda e, sA=sA, pr=pr: e.activation(
                                pt[:, (2 * pr) * 512:(2 * pr) * 512 + 512], sA[:], AF.Exp,
                                bias=bm8[:], scale=INV_SCALE))
                            p.bump("scalar", sc1, 1, lambda e, sB=sB, pr=pr: e.activation(
                                pt[:, (2 * pr + 1) * 512:(2 * pr + 1) * 512 + 512], sB[:], AF.Exp,
                                bias=bm8[:], scale=INV_SCALE))
                            exp_after[pr] = sc1.n
                            if upto == 41 and sc == 0 and qd == 0 and tt == 0 and pr == 0:
                                return _trunc(sc1, sc1.n)
                            p.wait("tensor", sc1, exp_after[pr])
                            if tt == 0:
                                p.wait("tensor", vc2, av_free)
                            avb = pb[4 + pr]
                            vbase = tt * 512 + cp * 128
                            p.bump("tensor", mm1, 1, lambda e, avb=avb, vbase=vbase, pr=pr, tt=tt: e.matmul(
                                avb[0:64, :], vtm[:, vbase:vbase + 64],
                                pt[:, (2 * pr) * 512:(2 * pr) * 512 + 512],
                                start=(tt == 0), stop=(tt == NT - 1), tile_position=(0, 0)))
                            p.bump("tensor", mm1, 1, lambda e, avb=avb, vbase=vbase, pr=pr, tt=tt: e.matmul(
                                avb[64:128, :], vtm[:, vbase + 64:vbase + 128],
                                pt[:, (2 * pr + 1) * 512:(2 * pr + 1) * 512 + 512],
                                start=(tt == 0), stop=(tt == NT - 1), tile_position=(0, 64),
                                skip_group_check=True))
                            if upto == 42 and sc == 0 and qd == 0 and tt == 0 and pr == 1:
                                return _trunc(mm1, mm1.n)
                        for hh in range(4):
                            p.bump("tensor", mm2, 1, lambda e, hh=hh, tt=tt: e.matmul(
                                pb[6][32 * hh:32 * hh + 1, :], ones_col[:],
                                pt[:, hh * 512:(hh + 1) * 512],
                                start=(tt == 0), stop=(tt == NT - 1), tile_position=(0, 32 * hh),
                                skip_group_check=(hh > 0)))
                        if upto == 43 and sc == 0 and qd == 0 and tt == 0:
                            return _trunc(mm2, mm2.n)
                    if upto == 440 and sc == 0 and qd == 0:
                        return _trunc(mm2, mm2.n)
                    p.wait("vector", mm2, mm2.n)
                    p.wait("vector", ld3, ld3.n)
                    for hh in range(4):
                        p.run("vector", lambda e, hh=hh: e.reciprocal(
                            rd4[32 * hh:32 * hh + 1, :], pb[6][32 * hh:32 * hh + 1, :]))
                        p.vdrain()
                    p.bump("vector", vc2, 1, lambda e: e.engine_nop())
                    if upto == 441 and sc == 0 and qd == 0:
                        return _trunc(vc2, vc2.n)
                    p.wait("sync", vc2, vc2.n)
                    p.wait("sync", mm2, mm2.n)
                    for hh in range(4):
                        p.bump("sync", ld3, 16, lambda e, hh=hh: e.dma_start(
                            out=rdz[0:1, hh * 512:(hh + 1) * 512],
                            in_=rd4[32 * hh:32 * hh + 1, :]))
                    for pr in range(2):
                        p.wait("tensor", ld3)
                        p.wait("tensor", vc1, vc1.n)
                        p.bump("tensor", mm2, 1, lambda e, pr=pr: e.matmul(
                            pb[7][:], e2b[0:1, 0:128], rdz[0:1, (2 * pr) * 512:(2 * pr) * 512 + 512],
                            start=True, stop=False, tile_position=(0, 0)))
                        p.bump("tensor", mm2, 1, lambda e, pr=pr: e.matmul(
                            pb[7][:], e2b[0:1, 128:256], rdz[0:1, (2 * pr + 1) * 512:(2 * pr + 1) * 512 + 512],
                            start=False, stop=True, tile_position=(0, 0)))
                        p.wait("vector", mm2)
                        p.bump("vector", vc1, 1, lambda e, pr=pr: e.tensor_copy(
                            bc2[:, pr * 512:(pr + 1) * 512], pb[7][:]))
                        p.vdrain()
                        if upto == 442 and sc == 0 and qd == 0 and pr == 0:
                            return _trunc(vc1, vc1.n)
                    p.wait("vector", mm1, mm1.n)
                    for pr in range(2):
                        p.wait("vector", st1, st1.n)
                        p.bump("vector", vc2, 1, lambda e, pr=pr: e.tensor_tensor(
                            fn0[:], pb[4 + pr][:], bc2[:, pr * 512:(pr + 1) * 512], ALU.mult))
                        p.vdrain()
                        p.wait("gpsimd", vc2)
                        strip = (2 * qd + pr) * 128
                        p.bump("gpsimd", st1, 16, lambda e, strip=strip, sc=sc: e.dma_start(
                            out=cc2i[strip:strip + 128, sc * 512:(sc + 1) * 512], in_=fn0[:]))
                    if upto == 44 and sc == 0 and qd == 0:
                        return _trunc(st1, st1.n)

            p.wait("gpsimd", st1)
            p.bump("gpsimd", cc0, 1, lambda e: e.collective_compute(
                "AllGather", ALU.bypass, replica_groups=RG4, ins=[cc2i], outs=[cc2o]))
            if upto == 4:
                return _trunc(cc0, 1)

            p.wait("gpsimd", st0, st0.n)
            p.bump("gpsimd", cc0, 1, lambda e: e.collective_compute(
                "AllGather", ALU.bypass, replica_groups=RG8, ins=[ccg_i], outs=[ccg_o]))
            p.bump("gpsimd", cc0, 1, lambda e: e.collective_compute(
                "AllGather", ALU.bypass, replica_groups=RG8, ins=[ccu_i], outs=[ccu_o]))

            for mc in range(NMC):
                p.wait("sync", st1)
                p.wait("sync", st0, st0.n)
                p.bump("sync", ld1, 16, lambda e, mc=mc: e.dma_start(
                    out=ttile2[:, 0:DWN], in_=dtmp[:, mc * 128:(mc + 1) * 128], transpose=True))
                p.wait("gpsimd", ld1)
                p.bump("gpsimd", st0, 16, lambda e, mc=mc: e.dma_start(
                    out=ccd_i[mc * 128:(mc + 1) * 128, :], in_=ttile2[:, 0:DWN]))
            p.wait("gpsimd", st0)
            p.bump("gpsimd", cc0, 1, lambda e: e.collective_compute(
                "AllGather", ALU.bypass, replica_groups=RG8, ins=[ccd_i], outs=[ccd_o]))

    x1s = sb("x1s", [128, 8 * F], BF16)
    with nc.sbuf_tensor("ffm", [128, NF * 512], BF16) as ffm, \
         nc.sbuf_tensor("wof", [128, 512], F32) as wof, \
         nc.sbuf_tensor("wo16", [128, NF * F], BF16) as wo16, \
         nc.sbuf_tensor("xtokt", [128, F], F32) as xtokt, \
         nc.sbuf_tensor("x1t", [128, F], F32) as x1t, \
         nc.sbuf_tensor("x1f", [128, F], F32) as x1f, \
         nc.sbuf_tensor("qx16", [128, F], BF16) as qx16:

        p.wait("sync", st1)
        p.wait("sync", st0, st0.n)
        for ci in range(NF):
            for half in range(4):
                p.wait("sync", vc1, vc1.n)
                p.bump("sync", ld1, 16, lambda e, ci=ci, half=half: e.dma_start(
                    out=wof[:], in_=wo[ci * 128:(ci + 1) * 128, half * 512:(half + 1) * 512]))
                p.wait("vector", ld1)
                p.bump("vector", vc1, 1, lambda e, ci=ci, half=half: e.tensor_copy(
                    wo16[:, ci * F + half * 512: ci * F + (half + 1) * 512], wof[:]))
                p.vdrain()

        def _mk_rank(e):
            p.holder["rank"] = e.cc_rank(RG4)
        p.em("sync", _mk_rank)
        p.wait("sync", cc0, 1)
        cc2o_r3 = cc2o.rearrange("(a p) s -> p a s", p=128)
        for h5 in range(2):
            p.wait("sync", mm0, mm0.n)
            p.bump("sync", ld2, 16, lambda e, h5=h5: e.dma_start(
                out=ffm.rearrange("p (a c) -> p a c", a=NF),
                in_=cc2o_r3[:, :, bass.ts(p.holder["rank"] * 2 + h5, 512)]))
            for ss in range(4):
                srow = h5 * 512 + ss * 128
                sidx = h5 * 4 + ss
                p.wait("sync", vc1, vc1.n)
                p.bump("sync", ld1, 16, lambda e, srow=srow: e.dma_start(
                    out=xtokt[:], in_=x_tok[srow:srow + 128, :]))
                p.wait("tensor", ld2)
                for j in range(4):
                    psx = pb[j % 2]
                    p.wait("tensor", vc1, vc1.n)
                    for ci in range(NF):
                        p.bump("tensor", mm0, 1, lambda e, psx=psx, ci=ci, j=j, ss=ss: e.matmul(
                            psx[:], ffm[:, ci * 512 + ss * 128: ci * 512 + ss * 128 + 128],
                            wo16[:, ci * F + j * 512: ci * F + (j + 1) * 512],
                            start=(ci == 0), stop=False))
                    p.bump("tensor", mm0, 1, lambda e, psx=psx, j=j: e.matmul(
                        psx[:], ones_row[:], bo16[0:1, j * 512:(j + 1) * 512],
                        start=False, stop=True))
                    p.wait("vector", mm0)
                    p.wait("vector", ld1)
                    p.wait("vector", sc0, sc0.n)
                    p.bump("vector", vc1, 1, lambda e, psx=psx, j=j: e.scalar_tensor_tensor(
                        x1t[:, j * 512:(j + 1) * 512], psx[:], 1.0, xtokt[:, j * 512:(j + 1) * 512],
                        ALU.mult, ALU.add, accum_out=lnsc[:, j:j + 1]))
                    p.vdrain()
                emit_ln(x1t[:], x1f[:])
                p.wait("vector", sc0)
                p.bump("vector", vc1, 1, lambda e, sidx=sidx: e.tensor_copy(
                    x1s[:, sidx * F:(sidx + 1) * F], x1f[:]))
                p.vdrain()
                emit_rowscale(128, x1f[:], save_scale=sx_all[:, sidx:sidx + 1])
                p.wait("vector", st1, st1.n)
                emit_quant(128, F, x1f[:], qx16[:])
                p.wait("gpsimd", vc0)
                p.bump("gpsimd", st1, 16, lambda e, srow=srow: e.dma_start(
                    out=qxd[srow:srow + 128, :], in_=qx16[:]))
    if upto == 5:
        return _trunc(st1, st1.n)

    def seg_iter(m0, m1):
        out = []
        while m0 < m1:
            s_ = m0 // GUP
            e_ = min(m1, (s_ + 1) * GUP)
            out.append((s_, m0 - s_ * GUP, e_ - m0))
            m0 = e_
        return out

    for h6 in range(2):
        with nc.sbuf_tensor(f"qxf{h6}", [128, NF * 512], BF16) as qxf, \
             nc.sbuf_tensor(f"grhs{h6}", [128, NF * 512], BF16) as grhs, \
             nc.sbuf_tensor(f"urhs{h6}", [128, NF * 512], BF16) as urhs, \
             nc.sbuf_tensor(f"gts{h6}", [128, 512], BF16) as gts, \
             nc.sbuf_tensor(f"uts{h6}", [128, 512], BF16) as uts, \
             nc.sbuf_tensor(f"tsb{h6}", [128, 4 * I], BF16) as tsb, \
             nc.sbuf_tensor(f"qt16_{h6}", [128, I], BF16) as qt16:

            if h6 == 0:
                p.wait("tensor", cc0, 3)
            p.wait("sync", cc0, 3)
            p.wait("sync", st1)
            p.wait("sync", vc1, vc1.n)
            p.wait("sync", mm2, mm2.n)
            p.wait("sync", mm1, mm1.n)
            for fc in range(NF):
                p.bump("sync", ld2, 16, lambda e, fc=fc, h6=h6, qxf=qxf: e.dma_start(
                    out=qxf[:, fc * 512:(fc + 1) * 512],
                    in_=qxd[h6 * 512:(h6 + 1) * 512, fc * 128:(fc + 1) * 128],
                    transpose=True))
            for mt in range(NMT):
                p.wait("sync", mm1, mm1.n)
                segs = seg_iter(mt * 512, (mt + 1) * 512)
                for fc in range(NF):
                    col = 0
                    for (s_, a_, w_) in segs:
                        p.bump("sync", ld3, 16, lambda e, fc=fc, s_=s_, a_=a_, w_=w_, col=col, grhs=grhs: e.dma_start(
                            out=grhs[:, fc * 512 + col: fc * 512 + col + w_],
                            in_=ccg_o[s_ * F + fc * 128: s_ * F + (fc + 1) * 128, a_:a_ + w_]))
                        p.bump("sync", ld3, 16, lambda e, fc=fc, s_=s_, a_=a_, w_=w_, col=col, urhs=urhs: e.dma_start(
                            out=urhs[:, fc * 512 + col: fc * 512 + col + w_],
                            in_=ccu_o[s_ * F + fc * 128: s_ * F + (fc + 1) * 128, a_:a_ + w_]))
                        col += w_
                p.wait("tensor", ld2)
                p.wait("tensor", ld3)
                for ss in range(4):
                    sidx = h6 * 4 + ss
                    gb = pb[ss % 2]
                    ub = pb[2 + ss % 2]
                    p.wait("tensor", sc1, sc1.n)
                    for ci in range(NF):
                        p.bump("tensor", mm1, 1, lambda e, gb=gb, ci=ci, ss=ss, qxf=qxf, grhs=grhs: e.matmul(
                            gb[:], qxf[:, ci * 512 + ss * 128: ci * 512 + ss * 128 + 128],
                            grhs[:, ci * 512:(ci + 1) * 512],
                            start=(ci == 0), stop=(ci == NF - 1)))
                    for ci in range(NF):
                        p.bump("tensor", mm1, 1, lambda e, ub=ub, ci=ci, ss=ss, qxf=qxf, urhs=urhs: e.matmul(
                            ub[:], qxf[:, ci * 512 + ss * 128: ci * 512 + ss * 128 + 128],
                            urhs[:, ci * 512:(ci + 1) * 512],
                            start=(ci == 0), stop=(ci == NF - 1)))
                    p.wait("scalar", mm1)
                    p.wait("scalar", vc1, vc1.n)
                    p.bump("scalar", sc1, 1, lambda e, gb=gb, sidx=sidx, gts=gts: e.activation(
                        gts[:], gb[:], AF.Silu, scale=sx_all[:, sidx:sidx + 1]))
                    p.bump("scalar", sc1, 1, lambda e, ub=ub, sidx=sidx, uts=uts: e.activation(
                        uts[:], ub[:], AF.Identity, scale=sx_all[:, sidx:sidx + 1]))
                    p.wait("vector", sc1)
                    p.bump("vector", vc1, 1, lambda e, ss=ss, mt=mt, tsb=tsb, gts=gts, uts=uts: e.tensor_tensor(
                        tsb[:, ss * I + mt * 512: ss * I + mt * 512 + 512], gts[:], uts[:], ALU.mult))
                    p.vdrain()
            p.wait("vector", out0, out0.n)
            p.wait("scalar", out0, out0.n)
            for ss in range(4):
                sidx = h6 * 4 + ss
                emit_rowscale(128, tsb[:, ss * I:(ss + 1) * I], save_scale=st_all[:, sidx:sidx + 1])
                p.wait("vector", st1, st1.n)
                for w0 in range(0, I, 2048):
                    w1 = min(w0 + 2048, I)
                    emit_quant(128, w1 - w0, tsb[:, ss * I + w0: ss * I + w1], qt16[:, w0:w1])
                p.wait("gpsimd", vc0)
                p.bump("gpsimd", st1, 16, lambda e, h6=h6, ss=ss, qt16=qt16: e.dma_start(
                    out=qtd[h6 * 512 + ss * 128: h6 * 512 + ss * 128 + 128, :], in_=qt16[:]))

        if upto == 6 and h6 == 0:
            return _trunc(st1, st1.n)

        with nc.sbuf_tensor(f"qtf{h6}", [128, NMC * 512], BF16) as qtf, \
             nc.sbuf_tensor(f"dnr{h6}", [128, NMC * 512], BF16) as dnr, \
             nc.sbuf_tensor(f"y2f{h6}", [128, 4 * F], BF16) as y2f:
            ytmp = qs3
            yout = qs4

            p.wait("sync", cc0, 4)
            p.wait("sync", st1)
            for mc in range(NMC):
                p.bump("sync", ld2, 16, lambda e, mc=mc, h6=h6, qtf=qtf: e.dma_start(
                    out=qtf[:, mc * 512:(mc + 1) * 512],
                    in_=qtd[h6 * 512:(h6 + 1) * 512, mc * 128:(mc + 1) * 128],
                    transpose=True))
            for jt in range(4):
                p.wait("sync", mm2, mm2.n)
                for mc in range(NMC):
                    for sj in range(2):
                        s_ = 2 * jt + sj
                        p.bump("sync", ld3, 16, lambda e, mc=mc, s_=s_, sj=sj, dnr=dnr: e.dma_start(
                            out=dnr[:, mc * 512 + sj * 256: mc * 512 + sj * 256 + 256],
                            in_=ccd_o[s_ * I + mc * 128: s_ * I + (mc + 1) * 128, :]))
                p.wait("tensor", ld2)
                p.wait("tensor", ld3)
                for ss in range(4):
                    sidx = h6 * 4 + ss
                    db = pb[4 + ss % 2]
                    p.wait("tensor", sc1, sc1.n)
                    for mc in range(NMC):
                        p.bump("tensor", mm2, 1, lambda e, db=db, mc=mc, ss=ss, qtf=qtf, dnr=dnr: e.matmul(
                            db[:], qtf[:, mc * 512 + ss * 128: mc * 512 + ss * 128 + 128],
                            dnr[:, mc * 512:(mc + 1) * 512],
                            start=(mc == 0), stop=(mc == NMC - 1)))
                    p.wait("scalar", mm2)
                    p.wait("scalar", vc1, vc1.n)
                    p.bump("scalar", sc1, 1, lambda e, db=db, ss=ss, jt=jt, sidx=sidx, y2f=y2f: e.activation(
                        y2f[:, ss * F + jt * 512: ss * F + jt * 512 + 512], db[:], AF.Identity,
                        scale=st_all[:, sidx:sidx + 1]))
            for ss in range(4):
                sidx = h6 * 4 + ss
                srow = h6 * 512 + ss * 128
                p.wait("vector", sc1, sc1.n)
                p.wait("vector", sc0, sc0.n)
                for j4 in range(4):
                    p.bump("vector", vc1, 1, lambda e, ss=ss, j4=j4, sidx=sidx, y2f=y2f: e.scalar_tensor_tensor(
                        ytmp[:, j4 * 512:(j4 + 1) * 512],
                        y2f[:, ss * F + j4 * 512: ss * F + j4 * 512 + 512], 1.0,
                        x1s[:, sidx * F + j4 * 512: sidx * F + j4 * 512 + 512],
                        ALU.mult, ALU.add, accum_out=lnsc[:, j4:j4 + 1]))
                    p.vdrain()
                p.wait("scalar", out0, out0.n)
                emit_ln(ytmp[:], yout[:])
                p.wait("gpsimd", sc0)
                p.bump("gpsimd", out0, 16, lambda e, srow=srow: e.dma_start(
                    out=y[srow:srow + 128, :], in_=yout[:]))

    p.wait("gpsimd", out0)

    return _finish()


_NC_CACHE = {}


def kernel(**inputs):
    x = np.ascontiguousarray(np.asarray(inputs["L0_attn_x"], np.float32))
    Wq = np.ascontiguousarray(np.asarray(inputs["L0_attn_Wq"], np.float32))
    Wk = np.ascontiguousarray(np.asarray(inputs["L0_attn_Wk"], np.float32))
    Wv = np.ascontiguousarray(np.asarray(inputs["L0_attn_Wv"], np.float32))
    Wo = np.ascontiguousarray(np.asarray(inputs["L0_attn_Wo"], np.float32))
    bq = np.asarray(inputs["L0_attn_bq"], np.float32)
    bk = np.asarray(inputs["L0_attn_bk"], np.float32)
    bv = np.asarray(inputs["L0_attn_bv"], np.float32)
    bo_ = np.asarray(inputs["L0_attn_bo"], np.float32)
    wg_ = np.ascontiguousarray(np.asarray(inputs["L0_ffn_w_gate"], np.float32))
    wu_ = np.ascontiguousarray(np.asarray(inputs["L0_ffn_w_up"], np.float32))
    wd_ = np.ascontiguousarray(np.asarray(inputs["L0_ffn_w_down"], np.float32))
    didx = np.asarray(inputs["L0_ffn_down_reorder_index"]).astype(np.int32).ravel()

    if "nc" not in _NC_CACHE:
        _NC_CACHE["nc"] = build_program()
    nc = _NC_CACHE["nc"]

    in_maps = []
    for c in range(8):
        b_ = c // 4
        g_ = c % 4
        cols = slice(g_ * HG, (g_ + 1) * HG)
        dslice = didx[c * GUP:(c + 1) * GUP]
        dpad = np.zeros(768, np.int32)
        dpad[:GUP] = dslice
        didx_l = np.ascontiguousarray(dpad.reshape(6, 128).T)
        bq_l = np.ascontiguousarray(bq[cols].reshape(4, 128).T)
        bk_l = np.ascontiguousarray(bk[cols].reshape(4, 128).T)
        bv_l = np.ascontiguousarray(bv[cols].reshape(4, 128).T)
        bqkv_l = np.concatenate([bq_l, bk_l, bv_l], axis=1)
        in_maps.append({
            "xb": x[b_],
            "x_tok": np.ascontiguousarray(x[b_, g_ * TOK:(g_ + 1) * TOK]),
            "wq": np.ascontiguousarray(Wq[:, cols]),
            "wk": np.ascontiguousarray(Wk[:, cols]),
            "wv": np.ascontiguousarray(Wv[:, cols]),
            "wo": Wo,
            "bqkv": np.ascontiguousarray(bqkv_l),
            "bo": np.ascontiguousarray(bo_.reshape(1, F)),
            "wg": wg_,
            "wu": wu_,
            "wd_my": np.ascontiguousarray(wd_[c * DWN:(c + 1) * DWN]),
            "didx_my": didx_l,
            "y": None,
        })
    for m in in_maps:
        del m["y"]

    res = run_bass_kernel_spmd(nc, in_maps, core_ids=list(range(8)))
    out = np.empty((B, S, F), np.float32)
    for c in range(8):
        b_ = c // 4
        g_ = c % 4
        out[b_, g_ * TOK:(g_ + 1) * TOK] = res.results[c]["y"]
    return out
